# version 09
# baseline (speedup 1.0000x reference)
import numpy as np
from contextlib import ExitStack
import concourse.bass as bass
import concourse.mybir as mybir
from concourse.bass_utils import run_bass_kernel_spmd

F32 = mybir.dt.float32
BF16 = mybir.dt.bfloat16
AF = mybir.ActivationFunctionType
ALU = mybir.AluOpType
AX = mybir.AxisListType

D = 1024
I = 2048
TT = 512
KD = D // 128
KI = I // 128
NE = 8
NSLOT = 54
SLOTC = 4096
RING = 6
W = TT + 3
NLANE = 4

ENGINES = ("tensor", "vector", "scalar", "gpsimd", "sync")


def slot_order():
    order = [("ip", j) for j in range(8)]
    order += [("cv", j) for j in range(4)]
    order += [("g", 0), ("u", 0), ("g", 1), ("u", 1)]
    order += [("fc", j) for j in range(4)]
    order += [("cp", j) for j in range(4)]
    order += [("ow", j) for j in range(4)]
    for ex in range(2, NE):
        order += [("g", ex), ("u", ex), ("d", ex - 2)]
    order += [("d", 6), ("d", 7)]
    order += [("sg", 0), ("sg", 1), ("su", 0), ("su", 1), ("sd", 0), ("sd", 1)]
    return order


SLOT_ORDER = slot_order()
SLOT = {k: idx for idx, k in enumerate(SLOT_ORDER)}
assert len(SLOT_ORDER) == NSLOT


class Plan:

    def __init__(self):
        self.ops = []
        self.writes = {}
        self.reads = {}
        self.dma_counts = {}
        self.dma_rr = {}

    @staticmethod
    def _overlap(recs, lo, hi):
        return [r for r in recs if r[0] < hi and lo < r[1]]

    def add(self, eng, emit, reads=(), writes=(), dma=None, tag=""):
        opid = len(self.ops)
        deps = set()
        for key, lo, hi in reads:
            for _, _, w in self._overlap(self.writes.get(key, ()), lo, hi):
                deps.add(w)
        for key, lo, hi in writes:
            for _, _, w in self._overlap(self.writes.get(key, ()), lo, hi):
                deps.add(w)
            for _, _, r in self._overlap(self.reads.get(key, ()), lo, hi):
                deps.add(r)
        dma_val = None
        dma_gate = None
        if dma is not None:
            lane = self.dma_rr.get(dma, 0)
            self.dma_rr[dma] = (lane + 1) % NLANE
            dma = f"{dma}{lane}"
            prev = self.dma_counts.get(dma, 0)
            if prev > 0:
                dma_gate = (dma, prev)
            self.dma_counts[dma] = prev + 16
            dma_val = self.dma_counts[dma]
        self.ops.append(dict(id=opid, eng=eng, emit=emit, deps=deps, dma=dma,
                             dma_val=dma_val, dma_gate=dma_gate, tag=tag))
        for key, lo, hi in writes:
            wr = self.writes.setdefault(key, [])
            wr[:] = [r for r in wr if not (lo <= r[0] and r[1] <= hi)]
            wr.append((lo, hi, opid))
            rd = self.reads.setdefault(key, [])
            rd[:] = [r for r in rd if not (lo <= r[0] and r[1] <= hi)]
        for key, lo, hi in reads:
            self.reads.setdefault(key, []).append((lo, hi, opid))
        return opid

    def finalize(self):
        ops = self.ops
        for op in ops:
            best = {}
            for d in op["deps"]:
                dop = ops[d]
                if (dop["eng"] == "tensor" and op["eng"] == "tensor"
                        and dop["dma"] is None and op["dma"] is None):
                    continue
                key = (dop["eng"], dop["dma"])
                if key not in best or d > best[key]:
                    best[key] = d
            op["deps"] = set(best.values())
        targets = set()
        for op in ops:
            for d in op["deps"]:
                if ops[d]["dma"] is None:
                    targets.add(d)
        rank = {}
        counts = {e: 0 for e in ENGINES}
        for op in ops:
            if op["dma"] is None and op["id"] in targets:
                counts[op["eng"]] += 1
                rank[op["id"]] = counts[op["eng"]]
                op["inc"] = True
            else:
                op["inc"] = op["dma"] is not None
        highwater = {e: {} for e in ENGINES}
        for op in ops:
            waits = []
            cands = []
            for d in sorted(op["deps"]):
                dop = ops[d]
                if dop["dma"] is not None:
                    cands.append((dop["dma"], dop["dma_val"]))
                else:
                    cands.append(("e_" + dop["eng"], rank[d]))
            if op["dma_gate"] is not None:
                cands.append(op["dma_gate"])
            for sem, val in cands:
                hw = highwater[op["eng"]]
                if hw.get(sem, 0) < val:
                    hw[sem] = val
                    waits.append((sem, val))
            op["waits"] = waits
        return counts


def build_nc(n_tiles):
    T = n_tiles * TT
    p = Plan()
    nc = bass.Bass("TRN2", target_bir_lowering=False)

    x_h = nc.dram_tensor("x", [T, D], F32, kind="ExternalInput").ap()
    wimg_h = nc.dram_tensor("wimg", [NSLOT, 128, SLOTC], F32, kind="ExternalInput").ap()
    cvec_h = nc.dram_tensor("cvec", [128, 80], F32, kind="ExternalInput").ap()
    wr_h = nc.dram_tensor("wrimg", [128, 64], F32, kind="ExternalInput").ap()
    y_h = nc.dram_tensor("y", [T, D], F32, kind="ExternalOutput").ap()
    wsc_h = nc.dram_tensor("wsc", [NSLOT, 128, SLOTC], BF16, kind="Internal").ap()

    ctx = ExitStack()
    e = ctx.enter_context
    xtok = e(nc.sbuf_tensor("xtok", [128, 2 * 4096], F32))
    xntok = e(nc.sbuf_tensor("xntok", [128, 4096], BF16))
    xnf = e(nc.sbuf_tensor("xnf", [128, 4096], BF16))
    hpre = e(nc.sbuf_tensor("hpre", [128, KI * W], BF16))
    gate = e(nc.sbuf_tensor("gate", [128, 8192], BF16))
    hbuf = e(nc.sbuf_tensor("hbuf", [128, 8192], BF16))
    tfc = e(nc.sbuf_tensor("tfc", [128, 4096], BF16))
    scr = e(nc.sbuf_tensor("scr", [128, 8192], BF16))
    yacc = e(nc.sbuf_tensor("yacc", [128, 4096], F32))
    sqscr = e(nc.sbuf_tensor("sqscr", [128, 2048], BF16))
    tmp = e(nc.sbuf_tensor("tmp", [128, 1024], F32))
    ring = e(nc.sbuf_tensor("ring", [128, RING * SLOTC], BF16))
    cvec = e(nc.sbuf_tensor("cvec_sb", [128, 80], F32))
    wrsb = e(nc.sbuf_tensor("wrsb", [128, 64], BF16))
    ones = e(nc.sbuf_tensor("ones_sb", [128, 2], BF16))
    onesrow = e(nc.sbuf_tensor("onesrow", [1, 128], BF16))
    iden = e(nc.sbuf_tensor("iden", [128, 128], BF16))
    iden32 = e(nc.sbuf_tensor("iden32", [128, 128], F32))
    eh = e(nc.sbuf_tensor("eh", [8, 1024], BF16))
    rows = e(nc.sbuf_tensor("rows", [1, 1536], F32))
    abbf = e(nc.sbuf_tensor("abbf", [1, 1024], BF16))
    lgsb = e(nc.sbuf_tensor("lgsb", [8, TT], F32))
    rtok = e(nc.sbuf_tensor("rtok", [128, 32], F32))
    msk = e(nc.sbuf_tensor("msk", [128, 64], F32))
    rwtok = e(nc.sbuf_tensor("rwtok", [128, 32], F32))
    rwf = e(nc.sbuf_tensor("rwf", [8, TT], BF16))
    rsc = e(nc.sbuf_tensor("rsc", [128, 24], F32))
    mln1 = e(nc.sbuf_tensor("mln1", [128, 4], F32))
    rln1 = e(nc.sbuf_tensor("rln1", [128, 4], F32))
    sln1 = e(nc.sbuf_tensor("sln1", [128, 4], F32))
    qln1 = e(nc.sbuf_tensor("qln1", [128, 4], F32))
    vsc = e(nc.sbuf_tensor("vsc", [128, 8], F32))
    consts = e(nc.sbuf_tensor("consts", [128, 4], F32))
    fbias = e(nc.sbuf_tensor("fbias", [128, 8], F32))
    lnbbf = e(nc.sbuf_tensor("lnbbf", [128, 16], BF16))
    PS = e(nc.psum_tensor("PS", [128, 3072], F32))
    PT = e(nc.psum_tensor("PT", [128, 2048], BF16))

    nc.const_aps.register(F32, 1e-5, consts[:, 0:1])
    nc.const_aps.register(F32, -0.5, consts[:, 1:2])
    nc.const_aps.register(F32, -1.0, consts[:, 2:3])

    sems = {}
    sem_names = ["e_tensor", "e_vector", "e_scalar", "e_gpsimd", "e_sync"]
    sem_names += [f"q_sync{i}" for i in range(NLANE)]
    sem_names += [f"q_gp{i}" for i in range(NLANE)]
    for name in sem_names:
        sems[name] = e(nc.semaphore(name))

    def R(t, lo, hi):
        return (t, lo, hi)

    rot = {"banks": [1024, 1536, 2048, 2560], "i": 0}

    def next_bank():
        b = rot["banks"][rot["i"] % len(rot["banks"])]
        rot["i"] += 1
        return b

    def mm_job(lhsT, rhs, out_lo, m_hi=128, tag=""):
        n = len(lhsT)
        for k in range(n):
            st, sp = (k == 0), (k == n - 1)
            la, lreg = lhsT[k]
            ra, rreg = rhs[k]
            p.add("tensor",
                  (lambda la=la, ra=ra, st=st, sp=sp, lo=out_lo, mh=m_hi:
                   nc.tensor.matmul(PS[0:mh, lo:lo + TT], la, ra, start=st, stop=sp)),
                  reads=[lreg, rreg], writes=[R("PS", out_lo, out_lo + TT)], tag=tag)

    def slot_base(s):
        return (s % RING) * SLOTC

    def load_slot(tile, key):
        s = SLOT[key]
        b = slot_base(s)
        if tile == 0:
            p.add("gpsimd",
                  (lambda s=s, b=b: nc.gpsimd.dma_start(
                      ring[:, b:b + SLOTC], wimg_h[s])),
                  reads=[R("wimg", s * SLOTC, (s + 1) * SLOTC)],
                  writes=[R("ring", b, b + SLOTC)], dma="q_gp", tag=f"cast{s}")
            p.add("sync",
                  (lambda s=s, b=b: nc.sync.dma_start(
                      wsc_h[s], ring[:, b:b + SLOTC])),
                  reads=[R("ring", b, b + SLOTC)],
                  writes=[R("wsc", s * SLOTC, (s + 1) * SLOTC)],
                  dma="q_sync", tag=f"wb{s}")
        else:
            p.add("sync",
                  (lambda s=s, b=b: nc.sync.dma_start(
                      ring[:, b:b + SLOTC], wsc_h[s])),
                  reads=[R("wsc", s * SLOTC, (s + 1) * SLOTC)],
                  writes=[R("ring", b, b + SLOTC)], dma="q_sync", tag=f"ld{s}")

    def slot_ap(key, lo, hi):
        b = slot_base(SLOT[key])
        return (ring[:, b + lo:b + hi], R("ring", b + lo, b + hi))

    def wslot(key, kchunks, ow, jj):
        return [slot_ap(key, c * ow + jj * 128, c * ow + (jj + 1) * 128)
                for c in range(kchunks)]

    def xnf_rhs():
        return [(xnf[:, c * TT:(c + 1) * TT], R("xnf", c * TT, (c + 1) * TT))
                for c in range(KD)]

    SG, UU, HE0 = 0, 2048, 4096

    def scr_reg(base, oc):
        return (scr[:, base + oc * TT:base + (oc + 1) * TT],
                R("scr", base + oc * TT, base + (oc + 1) * TT))

    p.add("sync", lambda: nc.sync.dma_start(cvec[:, :], cvec_h),
          writes=[R("cvec_sb", 0, 80)], dma="q_sync")
    p.add("gpsimd", lambda: nc.gpsimd.dma_start(wrsb[:, :], wr_h),
          writes=[R("wrsb", 0, 64)], dma="q_gp")
    p.add("gpsimd", lambda: nc.gpsimd.memset(ones[:, :], 1.0),
          writes=[R("ones_sb", 0, 2)])
    p.add("gpsimd", lambda: nc.gpsimd.memset(onesrow[:, :], 1.0),
          writes=[R("onesrow", 0, 128)])
    p.add("gpsimd", lambda: nc.gpsimd.memset(consts[:, 0:1], 1e-5),
          writes=[R("consts", 0, 1)])
    p.add("gpsimd", lambda: nc.gpsimd.memset(consts[:, 1:2], -0.5),
          writes=[R("consts", 1, 2)])
    p.add("gpsimd", lambda: nc.gpsimd.memset(consts[:, 2:3], -1.0),
          writes=[R("consts", 2, 3)])
    p.add("gpsimd", lambda: nc.gpsimd.iota(tmp[:, 0:128], [[0, 128]],
          channel_multiplier=1, allow_small_or_imprecise_dtypes=True),
          writes=[R("tmp", 0, 128)])
    p.add("gpsimd", lambda: nc.gpsimd.iota(tmp[:, 512:640], [[1, 128]],
          channel_multiplier=0, allow_small_or_imprecise_dtypes=True),
          writes=[R("tmp", 512, 640)])
    p.add("vector", lambda: nc.vector.tensor_tensor(
        iden[:, :], tmp[:, 0:128], tmp[:, 512:640], ALU.is_equal),
        reads=[R("tmp", 0, 128), R("tmp", 512, 640)], writes=[R("iden", 0, 128)])
    p.add("vector", lambda: nc.vector.tensor_tensor(
        iden32[:, :], tmp[:, 0:128], tmp[:, 512:640], ALU.is_equal),
        reads=[R("tmp", 0, 128), R("tmp", 512, 640)], writes=[R("iden32", 0, 128)])
    p.add("gpsimd", lambda: nc.gpsimd.iota(tmp[0:8, 0:1024], [[1, 8], [0, 128]],
          channel_multiplier=0, allow_small_or_imprecise_dtypes=True),
          writes=[R("tmp", 0, 1024)])
    p.add("gpsimd", lambda: nc.gpsimd.iota(yacc[0:8, 0:1024], [[0, 8], [0, 128]],
          channel_multiplier=1, allow_small_or_imprecise_dtypes=True),
          writes=[R("yacc", 0, 1024)])
    p.add("vector", lambda: nc.vector.tensor_tensor(
        eh[:, :], tmp[0:8, 0:1024], yacc[0:8, 0:1024], ALU.is_equal),
        reads=[R("tmp", 0, 1024), R("yacc", 0, 1024)], writes=[R("eh", 0, 1024)])
    p.add("gpsimd", lambda: nc.gpsimd.memset(
        hpre[:, :].rearrange("p (c w) -> p c w", w=W)[:, :, 0:3], 0.0),
        writes=[R("hpre", 0, KI * W)])

    def load_x(i):
        par = i % 2
        src = x_h[i * TT:(i + 1) * TT, :].rearrange("(s p) d -> p s d", p=128)
        dst = xtok[:, par * 4096:(par + 1) * 4096].rearrange(
            "p (s d) -> p s d", d=D)
        p.add("sync",
              (lambda src=src, dst=dst: nc.sync.dma_start(dst, src)),
              writes=[R("xtok", par * 4096, (par + 1) * 4096)],
              dma="q_sync", tag=f"x{i}")

    def xreg(i, s):
        par = i % 2
        return R("xtok", par * 4096 + s * 1024, par * 4096 + (s + 1) * 1024)

    def xsub(i, s):
        par = i % 2
        return xtok[:, par * 4096 + s * 1024: par * 4096 + (s + 1) * 1024]

    def ln1_stats(i):
        for s in range(4):
            p.add("vector",
                  (lambda i=i, s=s: nc.vector.reduce_sum(
                      sln1[:, s:s + 1], xsub(i, s), AX.X)),
                  reads=[xreg(i, s)], writes=[R("sln1", s, s + 1)])
            p.add("scalar",
                  (lambda i=i, s=s: nc.scalar.activation(
                      sqscr[:, 0:1024], xsub(i, s), AF.Square,
                      accum_out=qln1[:, s:s + 1])),
                  reads=[xreg(i, s)],
                  writes=[R("sqscr", 0, 1024), R("qln1", s, s + 1)])
        p.add("vector", lambda: nc.vector.tensor_scalar_mul(
            mln1[:, 0:4], sln1[:, 0:4], 1.0 / D),
            reads=[R("sln1", 0, 4)], writes=[R("mln1", 0, 4)])
        p.add("vector", lambda: nc.vector.tensor_tensor(
            vsc[:, 0:4], mln1[:, 0:4], mln1[:, 0:4], ALU.mult),
            reads=[R("mln1", 0, 4)], writes=[R("vsc", 0, 4)])
        p.add("vector", lambda: nc.vector.scalar_tensor_tensor(
            vsc[:, 4:8], qln1[:, 0:4], 1.0 / D, vsc[:, 0:4],
            ALU.mult, ALU.subtract),
            reads=[R("qln1", 0, 4), R("vsc", 0, 4)], writes=[R("vsc", 4, 8)])
        p.add("scalar", lambda: nc.scalar.activation(
            rln1[:, 0:4], vsc[:, 4:8], AF.Ln, bias=1e-5),
            reads=[R("vsc", 4, 8), R("consts", 0, 1)], writes=[R("rln1", 0, 4)])
        p.add("scalar", lambda: nc.scalar.activation(
            rln1[:, 0:4], rln1[:, 0:4], AF.Exp, scale=-0.5),
            reads=[R("rln1", 0, 4), R("consts", 1, 2)], writes=[R("rln1", 0, 4)])

    def ln1_apply(i):
        for s in range(4):
            p.add("vector",
                  (lambda i=i, s=s: nc.vector.tensor_scalar(
                      xntok[:, s * 1024:(s + 1) * 1024], xsub(i, s),
                      mln1[:, s:s + 1], rln1[:, s:s + 1],
                      ALU.subtract, ALU.mult)),
                  reads=[xreg(i, s), R("mln1", s, s + 1), R("rln1", s, s + 1)],
                  writes=[R("xntok", s * 1024, (s + 1) * 1024)])
        for s in range(4):
            half = s % 2
            for c in range(8):
                p.add("tensor",
                      (lambda s=s, c=c, half=half: nc.tensor.transpose(
                          PT[:, half * 1024 + c * 128: half * 1024 + (c + 1) * 128],
                          xntok[:, s * 1024 + c * 128: s * 1024 + (c + 1) * 128],
                          iden[:, :])),
                      reads=[R("xntok", s * 1024 + c * 128, s * 1024 + (c + 1) * 128),
                             R("iden", 0, 128)],
                      writes=[R("PT", half * 1024 + c * 128,
                                half * 1024 + (c + 1) * 128)])
            out_ap = xnf[:, :].rearrange("p (c sb t) -> p c sb t",
                                         c=8, sb=4, t=128)[:, :, s, :]
            in_ap = PT[:, half * 1024:(half + 1) * 1024].rearrange(
                "p (c t) -> p c t", t=128)
            p.add("scalar",
                  (lambda out_ap=out_ap, in_ap=in_ap:
                   nc.scalar.copy(out_ap, in_ap)),
                  reads=[R("PT", half * 1024, (half + 1) * 1024)],
                  writes=[R("xnf", 0, 4096)])

    load_x(0)
    ln1_stats(0)
    ln1_apply(0)

    pend_yfin = [None]
    for i in range(n_tiles):
        rot["banks"] = [1024, 1536, 2048, 2560]

        for oc in range(32):
            if oc % 4 == 0:
                load_slot(i, ("ip", oc // 4))
            lo = next_bank()
            mm_job(wslot(("ip", oc // 4), KD, 512, oc % 4), xnf_rhs(), lo,
                   tag=f"ip{oc}")
            if oc < 16:
                p.add("scalar",
                      (lambda oc=oc, lo=lo: nc.scalar.copy(
                          hpre[:, oc * W + 3: oc * W + 3 + TT], PS[:, lo:lo + TT])),
                      reads=[R("PS", lo, lo + TT)],
                      writes=[R("hpre", oc * W + 3, oc * W + 3 + TT)])
            else:
                g = oc - 16
                p.add("scalar",
                      (lambda g=g, lo=lo: nc.scalar.activation(
                          gate[:, g * TT:(g + 1) * TT], PS[:, lo:lo + TT], AF.Silu)),
                      reads=[R("PS", lo, lo + TT)],
                      writes=[R("gate", g * TT, (g + 1) * TT)])

        if pend_yfin[0] is not None:
            pend_yfin[0]()
            pend_yfin[0] = None

        def stat_chunk(c):
            sq = c % 4
            p.add("scalar",
                  (lambda c=c, sq=sq: nc.scalar.activation(
                      sqscr[:, sq * TT:(sq + 1) * TT],
                      hbuf[:, c * TT:(c + 1) * TT], AF.Square)),
                  reads=[R("hbuf", c * TT, (c + 1) * TT)],
                  writes=[R("sqscr", sq * TT, (sq + 1) * TT)])
            p.add("tensor",
                  (lambda c=c: nc.tensor.matmul(
                      PS[0:1, 0:512], ones[:, 0:1], hbuf[:, c * TT:(c + 1) * TT],
                      start=(c == 0), stop=(c == KI - 1))),
                  reads=[R("ones_sb", 0, 1), R("hbuf", c * TT, (c + 1) * TT)],
                  writes=[R("PS", 0, 512)], tag="S1")
            p.add("tensor",
                  (lambda c=c, sq=sq: nc.tensor.matmul(
                      PS[0:1, 512:1024], ones[:, 0:1],
                      sqscr[:, sq * TT:(sq + 1) * TT],
                      start=(c == 0), stop=(c == KI - 1))),
                  reads=[R("ones_sb", 0, 1), R("sqscr", sq * TT, (sq + 1) * TT)],
                  writes=[R("PS", 512, 1024)], tag="S2")

        for oc in range(KI):
            if oc % 4 == 0:
                load_slot(i, ("cv", oc // 4))
            g = oc // 2
            oc2 = oc % 2
            og = g % 2
            lh, rh = [], []
            for cc in range(2):
                gc = 2 * g + cc
                for k in range(4):
                    lh.append(slot_ap(("cv", g // 2),
                                      cc * 2048 + k * 512 + og * 256 + oc2 * 128,
                                      cc * 2048 + k * 512 + og * 256 + (oc2 + 1) * 128))
                    rh.append((hpre[:, gc * W + k: gc * W + k + TT],
                               R("hpre", gc * W + k, gc * W + k + TT)))
            lo = next_bank()
            mm_job(lh, rh, lo, tag=f"cv{oc}")
            p.add("vector",
                  (lambda oc=oc, lo=lo: nc.vector.scalar_tensor_tensor(
                      hbuf[:, oc * TT:(oc + 1) * TT], PS[:, lo:lo + TT],
                      cvec[:, oc:oc + 1], gate[:, oc * TT:(oc + 1) * TT],
                      ALU.add, ALU.mult)),
                  reads=[R("PS", lo, lo + TT), R("cvec_sb", oc, oc + 1),
                         R("gate", oc * TT, (oc + 1) * TT)],
                  writes=[R("hbuf", oc * TT, (oc + 1) * TT)])
            if oc >= 3:
                stat_chunk(oc - 3)
        for c_ in range(KI - 3, KI):
            stat_chunk(c_)

        if i + 1 < n_tiles:
            load_x(i + 1)
            src = hpre[:, :].rearrange("p (c w) -> p c w", w=W)[:, :, TT:TT + 3]
            dst = hpre[:, :].rearrange("p (c w) -> p c w", w=W)[:, :, 0:3]
            p.add("vector",
                  (lambda src=src, dst=dst: nc.vector.tensor_copy(dst, src)),
                  reads=[R("hpre", 0, KI * W)], writes=[R("hpre", 0, KI * W)],
                  tag="halo")

        lo_lg = next_bank()
        mm_job([(wrsb[:, c * 8:(c + 1) * 8], R("wrsb", c * 8, (c + 1) * 8))
                for c in range(KD)], xnf_rhs(), lo_lg, m_hi=8, tag="lg")
        p.add("scalar",
              (lambda lo=lo_lg: nc.scalar.activation(
                  lgsb[:, :], PS[0:8, lo:lo + TT], AF.Identity,
                  bias=cvec[0:8, 72:73], scale=1.0)),
              reads=[R("PS", lo_lg, lo_lg + TT), R("cvec_sb", 72, 73)],
              writes=[R("lgsb", 0, TT)])
        lo_t = next_bank()
        for s in range(4):
            p.add("tensor",
                  (lambda s=s, lo=lo_t: nc.tensor.transpose(
                      PS[:, lo + s * 8:lo + (s + 1) * 8],
                      lgsb[0:8, s * 128:(s + 1) * 128], iden32[0:8, 0:8])),
                  reads=[R("lgsb", s * 128, (s + 1) * 128), R("iden32", 0, 8)],
                  writes=[R("PS", lo_t + s * 8, lo_t + (s + 1) * 8)])
        p.add("vector",
              (lambda lo=lo_t: nc.vector.tensor_copy(rtok[:, :], PS[:, lo:lo + 32])),
              reads=[R("PS", lo_t, lo_t + 32)], writes=[R("rtok", 0, 32)])

        p.add("vector", lambda: nc.vector.tensor_scalar_mul(
            rows[:, 0:512], PS[0:1, 0:512], 1.0 / I),
            reads=[R("PS", 0, 512)], writes=[R("rows", 0, 512)])
        p.add("vector", lambda: nc.vector.tensor_tensor(
            rows[:, 1024:1536], rows[:, 0:512], rows[:, 0:512], ALU.mult),
            reads=[R("rows", 0, 512)], writes=[R("rows", 1024, 1536)])
        p.add("vector", lambda: nc.vector.scalar_tensor_tensor(
            rows[:, 512:1024], PS[0:1, 512:1024], 1.0 / I, rows[:, 1024:1536],
            ALU.mult, ALU.subtract),
            reads=[R("PS", 512, 1024), R("rows", 1024, 1536)],
            writes=[R("rows", 512, 1024)])
        p.add("scalar", lambda: nc.scalar.activation(
            rows[:, 1024:1536], rows[:, 512:1024], AF.Ln, bias=1e-5),
            reads=[R("rows", 512, 1024), R("consts", 0, 1)],
            writes=[R("rows", 1024, 1536)])
        p.add("scalar", lambda: nc.scalar.activation(
            rows[:, 512:1024], rows[:, 1024:1536], AF.Exp, scale=-0.5),
            reads=[R("rows", 1024, 1536), R("consts", 1, 2)],
            writes=[R("rows", 512, 1024)])
        p.add("scalar", lambda: nc.scalar.copy(abbf[:, 0:512], rows[:, 512:1024]),
              reads=[R("rows", 512, 1024)], writes=[R("abbf", 0, 512)])
        p.add("vector", lambda: nc.vector.scalar_tensor_tensor(
            abbf[:, 512:1024], rows[:, 0:512], -1.0, rows[:, 512:1024],
            ALU.mult, ALU.mult),
            reads=[R("rows", 0, 1024)], writes=[R("abbf", 512, 1024)])

        for oc in range(4):
            if oc == 0:
                load_slot(i, ("g", 0))
            lo = next_bank()
            mm_job(wslot(("g", 0), KD, 512, oc), xnf_rhs(), lo, tag=f"g0.{oc}")
            dst, dreg = scr_reg(HE0, oc)
            p.add("scalar",
                  (lambda dst=dst, lo=lo: nc.scalar.activation(
                      dst, PS[:, lo:lo + TT], AF.Silu)),
                  reads=[R("PS", lo, lo + TT)], writes=[dreg])

        p.add("tensor", lambda: nc.tensor.matmul(
            PS[:, 0:512], onesrow[0:1, :], abbf[0:1, 0:512], start=True, stop=True),
            reads=[R("onesrow", 0, 128), R("abbf", 0, 512)],
            writes=[R("PS", 0, 512)], tag="bcA")
        p.add("tensor", lambda: nc.tensor.matmul(
            PS[:, 512:1024], onesrow[0:1, :], abbf[0:1, 512:1024],
            start=True, stop=True),
            reads=[R("onesrow", 0, 128), R("abbf", 512, 1024)],
            writes=[R("PS", 512, 1024)], tag="bcB")
        for oc in range(4):
            if oc == 0:
                load_slot(i, ("u", 0))
            lo = next_bank()
            mm_job(wslot(("u", 0), KD, 512, oc), xnf_rhs(), lo, tag=f"u0.{oc}")
            dst, dreg = scr_reg(SG, oc)
            p.add("scalar",
                  (lambda dst=dst, lo=lo: nc.scalar.copy(dst, PS[:, lo:lo + TT])),
                  reads=[R("PS", lo, lo + TT)], writes=[dreg])
        for oc in range(4):
            if oc == 0:
                load_slot(i, ("g", 1))
            lo = next_bank()
            mm_job(wslot(("g", 1), KD, 512, oc), xnf_rhs(), lo, tag=f"g1.{oc}")
            dst, dreg = scr_reg(HE0 + 2048, oc)
            p.add("scalar",
                  (lambda dst=dst, lo=lo: nc.scalar.activation(
                      dst, PS[:, lo:lo + TT], AF.Silu)),
                  reads=[R("PS", lo, lo + TT)], writes=[dreg])
        for oc in range(4):
            if oc == 0:
                load_slot(i, ("u", 1))
            lo = next_bank()
            mm_job(wslot(("u", 1), KD, 512, oc), xnf_rhs(), lo, tag=f"u1.{oc}")
            dst, dreg = scr_reg(UU, oc)
            p.add("scalar",
                  (lambda dst=dst, lo=lo: nc.scalar.copy(dst, PS[:, lo:lo + TT])),
                  reads=[R("PS", lo, lo + TT)], writes=[dreg])

        for c in range(KI):
            s2 = c % 2
            p.add("vector",
                  (lambda c=c, s2=s2: nc.vector.tensor_tensor(
                      tmp[:, s2 * 512:(s2 + 1) * 512], hbuf[:, c * TT:(c + 1) * TT],
                      PS[:, 0:512], ALU.mult)),
                  reads=[R("hbuf", c * TT, (c + 1) * TT), R("PS", 0, 512)],
                  writes=[R("tmp", s2 * 512, (s2 + 1) * 512)])
            p.add("vector",
                  (lambda c=c, s2=s2: nc.vector.tensor_tensor(
                      hpre[:, c * W + 3: c * W + 3 + TT],
                      tmp[:, s2 * 512:(s2 + 1) * 512], PS[:, 512:1024], ALU.add)),
                  reads=[R("tmp", s2 * 512, (s2 + 1) * 512), R("PS", 512, 1024)],
                  writes=[R("hpre", c * W + 3, c * W + 3 + TT)])

        if i == 0:
            p.add("vector", lambda: nc.vector.tensor_copy(
                lnbbf[:, :], cvec[:, 32:48]),
                reads=[R("cvec_sb", 32, 48)], writes=[R("lnbbf", 0, 16)])
            for j in range(4):
                sidx = SLOT[("fc", j)]
                bb = slot_base(sidx)
                p.add("gpsimd",
                      (lambda sidx=sidx, bb=bb: nc.gpsimd.dma_start(
                          ring[:, bb:bb + SLOTC], wimg_h[sidx])),
                      reads=[R("wimg", sidx * SLOTC, (sidx + 1) * SLOTC)],
                      writes=[R("ring", bb, bb + SLOTC)], dma="q_gp",
                      tag=f"cast{sidx}")
                for jj2 in range(2):
                    oc2 = 2 * j + jj2
                    lo = next_bank()
                    for c in range(KI):
                        la, lreg = slot_ap(("fc", j), c * 256 + jj2 * 128,
                                           c * 256 + (jj2 + 1) * 128)
                        p.add("tensor",
                              (lambda la=la, c=c, lo=lo:
                               nc.tensor.matmul(PS[0:128, lo:lo + 1], la,
                                                lnbbf[:, c:c + 1],
                                                start=(c == 0), stop=(c == KI - 1))),
                              reads=[lreg, R("lnbbf", c, c + 1)],
                              writes=[R("PS", lo, lo + TT)], tag=f"c1.{oc2}")
                    p.add("vector",
                          (lambda oc2=oc2, lo=lo: nc.vector.scalar_tensor_tensor(
                              fbias[:, oc2:oc2 + 1], PS[0:128, lo:lo + 1], 1.0,
                              cvec[:, 48 + oc2:49 + oc2], ALU.mult, ALU.add)),
                          reads=[R("PS", lo, lo + TT),
                                 R("cvec_sb", 48 + oc2, 49 + oc2)],
                          writes=[R("fbias", oc2, oc2 + 1)])
                for c in range(KI):
                    lo2 = c * 256
                    p.add("vector",
                          (lambda j=j, c=c, lo2=lo2: nc.vector.tensor_scalar_mul(
                              slot_ap(("fc", j), lo2, lo2 + 256)[0],
                              slot_ap(("fc", j), lo2, lo2 + 256)[0],
                              cvec[:, 16 + c:17 + c])),
                          reads=[R("ring", bb + lo2, bb + lo2 + 256),
                                 R("cvec_sb", 16 + c, 17 + c)],
                          writes=[R("ring", bb + lo2, bb + lo2 + 256)])
                p.add("sync",
                      (lambda sidx=sidx, bb=bb: nc.sync.dma_start(
                          wsc_h[sidx], ring[:, bb:bb + SLOTC])),
                      reads=[R("ring", bb, bb + SLOTC)],
                      writes=[R("wsc", sidx * SLOTC, (sidx + 1) * SLOTC)],
                      dma="q_sync", tag=f"wb{sidx}")
        hn_rhs = [(hpre[:, c * W + 3: c * W + 3 + TT],
                   R("hpre", c * W + 3, c * W + 3 + TT)) for c in range(KI)]
        for oc in range(8):
            if i != 0 and oc % 2 == 0:
                load_slot(i, ("fc", oc // 2))
            lo = next_bank()
            mm_job(wslot(("fc", oc // 2), KI, 256, oc % 2), hn_rhs, lo,
                   tag=f"fc{oc}")
            p.add("scalar",
                  (lambda oc=oc, lo=lo: nc.scalar.activation(
                      tfc[:, oc * TT:(oc + 1) * TT], PS[:, lo:lo + TT], AF.Silu,
                      bias=fbias[:, oc:oc + 1])),
                  reads=[R("PS", lo, lo + TT), R("fbias", oc, oc + 1)],
                  writes=[R("tfc", oc * TT, (oc + 1) * TT)])

        rot["banks"] = [1024, 1536, 2048, 2560, 512]

        tfc_rhs = [(tfc[:, c * TT:(c + 1) * TT], R("tfc", c * TT, (c + 1) * TT))
                   for c in range(KD)]
        for oc in range(KI):
            if oc % 4 == 0:
                load_slot(i, ("cp", oc // 4))
            lo = next_bank()
            mm_job(wslot(("cp", oc // 4), KD, 512, oc % 4), tfc_rhs, lo,
                   tag=f"cp{oc}")
            p.add("vector",
                  (lambda oc=oc, lo=lo: nc.vector.scalar_tensor_tensor(
                      gate[:, oc * TT:(oc + 1) * TT], PS[:, lo:lo + TT],
                      cvec[:, 56 + oc:57 + oc], hbuf[:, oc * TT:(oc + 1) * TT],
                      ALU.add, ALU.add)),
                  reads=[R("PS", lo, lo + TT), R("cvec_sb", 56 + oc, 57 + oc),
                         R("hbuf", oc * TT, (oc + 1) * TT)],
                  writes=[R("gate", oc * TT, (oc + 1) * TT)])

        rt3 = rtok[:, :].rearrange("p (s e) -> p s e", e=8)
        l23 = msk[:, 32:64].rearrange("p (s e) -> p s e", e=8)
        mk3 = msk[:, 0:32].rearrange("p (s e) -> p s e", e=8)
        rw3 = rwtok[:, :].rearrange("p (s e) -> p s e", e=8)
        m1b = rsc[:, 0:4].rearrange("p (s o) -> p s o", o=1).broadcast_to([128, 4, 8])
        m2b = rsc[:, 4:8].rearrange("p (s o) -> p s o", o=1).broadcast_to([128, 4, 8])
        p1b = rsc[:, 16:20].rearrange("p (s o) -> p s o", o=1).broadcast_to([128, 4, 8])
        p2b = rsc[:, 20:24].rearrange("p (s o) -> p s o", o=1).broadcast_to([128, 4, 8])
        p.add("vector", lambda: nc.vector.tensor_reduce(
            rsc[:, 0:4], rt3, AX.X, ALU.max),
            reads=[R("rtok", 0, 32)], writes=[R("rsc", 0, 4)])
        p.add("vector", lambda: nc.vector.tensor_tensor(
            mk3, rt3, m1b, ALU.is_equal),
            reads=[R("rtok", 0, 32), R("rsc", 0, 4)], writes=[R("msk", 0, 32)])
        p.add("vector", lambda: nc.vector.scalar_tensor_tensor(
            msk[:, 32:64], msk[:, 0:32], -1e30, rtok[:, :], ALU.mult, ALU.add),
            reads=[R("msk", 0, 32), R("rtok", 0, 32)], writes=[R("msk", 32, 64)])
        p.add("vector", lambda: nc.vector.tensor_reduce(
            rsc[:, 4:8], l23, AX.X, ALU.max),
            reads=[R("msk", 32, 64)], writes=[R("rsc", 4, 8)])
        p.add("vector", lambda: nc.vector.tensor_tensor(
            l23, l23, m2b, ALU.is_equal),
            reads=[R("msk", 32, 64), R("rsc", 4, 8)], writes=[R("msk", 32, 64)])
        p.add("vector", lambda: nc.vector.tensor_sub(
            rsc[:, 8:12], rsc[:, 4:8], rsc[:, 0:4]),
            reads=[R("rsc", 0, 8)], writes=[R("rsc", 8, 12)])
        p.add("scalar", lambda: nc.scalar.activation(
            rsc[:, 12:16], rsc[:, 8:12], AF.Exp),
            reads=[R("rsc", 8, 12)], writes=[R("rsc", 12, 16)])
        if i + 1 < n_tiles:
            ln1_stats(i + 1)
        p.add("vector", lambda: nc.vector.tensor_scalar_add(
            rsc[:, 16:20], rsc[:, 12:16], 1.0),
            reads=[R("rsc", 12, 16)], writes=[R("rsc", 16, 20)])
        p.add("vector", lambda: nc.vector.reciprocal(
            rsc[:, 16:20], rsc[:, 16:20]),
            reads=[R("rsc", 16, 20)], writes=[R("rsc", 16, 20)])
        p.add("vector", lambda: nc.vector.tensor_tensor(
            rsc[:, 20:24], rsc[:, 12:16], rsc[:, 16:20], ALU.mult),
            reads=[R("rsc", 12, 20)], writes=[R("rsc", 20, 24)])
        p.add("vector", lambda: nc.vector.tensor_tensor(
            rw3, mk3, p1b, ALU.mult),
            reads=[R("msk", 0, 32), R("rsc", 16, 20)], writes=[R("rwtok", 0, 32)])
        p.add("vector", lambda: nc.vector.tensor_tensor(
            mk3, l23, p2b, ALU.mult),
            reads=[R("msk", 32, 64), R("rsc", 20, 24)], writes=[R("msk", 0, 32)])
        p.add("vector", lambda: nc.vector.tensor_add(
            rwtok[:, :], rwtok[:, :], msk[:, 0:32]),
            reads=[R("rwtok", 0, 32), R("msk", 0, 32)], writes=[R("rwtok", 0, 32)])

        h2_rhs = [(gate[:, c * TT:(c + 1) * TT], R("gate", c * TT, (c + 1) * TT))
                  for c in range(KI)]
        for oc in range(8):
            if oc % 2 == 0:
                load_slot(i, ("ow", oc // 2))
            lo = next_bank()
            mm_job(wslot(("ow", oc // 2), KI, 256, oc % 2), h2_rhs, lo,
                   tag=f"ow{oc}")
            p.add("scalar",
                  (lambda oc=oc, lo=lo: nc.scalar.copy(
                      yacc[:, oc * TT:(oc + 1) * TT], PS[:, lo:lo + TT])),
                  reads=[R("PS", lo, lo + TT)],
                  writes=[R("yacc", oc * TT, (oc + 1) * TT)])

        lo_t2 = next_bank()
        for s in range(4):
            p.add("tensor",
                  (lambda s=s, lo=lo_t2: nc.tensor.transpose(
                      PS[0:8, lo + s * 128:lo + (s + 1) * 128],
                      rwtok[:, s * 8:(s + 1) * 8], iden32[:, :])),
                  reads=[R("rwtok", s * 8, (s + 1) * 8), R("iden32", 0, 128)],
                  writes=[R("PS", lo_t2 + s * 128, lo_t2 + (s + 1) * 128)])
        p.add("scalar",
              (lambda lo=lo_t2: nc.scalar.copy(rwf[:, :], PS[0:8, lo:lo + TT])),
              reads=[R("PS", lo_t2, lo_t2 + TT)], writes=[R("rwf", 0, TT)])

        def emit_rw_tt(ex, ubase):
            heb = HE0 + (ex % 2) * 2048
            p.add("tensor",
                  (lambda ex=ex: nc.tensor.matmul(
                      PS[:, 0:512], eh[:, ex * 128:(ex + 1) * 128], rwf[0:8, :],
                      start=True, stop=True)),
                  reads=[R("eh", ex * 128, (ex + 1) * 128), R("rwf", 0, TT)],
                  writes=[R("PS", 0, 512)], tag=f"rw{ex}")
            for oc in range(4):
                ha, hreg = scr_reg(heb, oc)
                p.add("vector",
                      (lambda ha=ha, hreg=hreg: nc.vector.tensor_tensor(
                          ha, ha, PS[:, 0:512], ALU.mult)),
                      reads=[hreg, R("PS", 0, 512)], writes=[hreg])
            for oc in range(4):
                ha, hreg = scr_reg(heb, oc)
                ua, ureg = scr_reg(ubase, oc)
                p.add("vector",
                      (lambda ha=ha, ua=ua, hreg=hreg, ureg=ureg:
                       nc.vector.tensor_tensor(ha, ha, ua, ALU.mult)),
                      reads=[hreg, ureg], writes=[hreg])

        def emit_wd(ex):
            heb = HE0 + (ex % 2) * 2048
            he_rhs = [scr_reg(heb, c) for c in range(4)]
            for oc in range(8):
                if oc == 0:
                    load_slot(i, ("d", ex))
                lo = next_bank()
                mm_job(wslot(("d", ex), 4, 1024, oc), he_rhs, lo, tag=f"d{ex}.{oc}")
                p.add("vector",
                      (lambda oc=oc, lo=lo: nc.vector.tensor_tensor(
                          yacc[:, oc * TT:(oc + 1) * TT],
                          yacc[:, oc * TT:(oc + 1) * TT],
                          PS[:, lo:lo + TT], ALU.add)),
                      reads=[R("yacc", oc * TT, (oc + 1) * TT), R("PS", lo, lo + TT)],
                      writes=[R("yacc", oc * TT, (oc + 1) * TT)])

        emit_rw_tt(0, SG)
        emit_rw_tt(1, UU)
        for ex in range(2, NE):
            heb = HE0 + (ex % 2) * 2048
            for oc in range(4):
                if oc == 0:
                    load_slot(i, ("g", ex))
                lo = next_bank()
                mm_job(wslot(("g", ex), KD, 512, oc), xnf_rhs(), lo,
                       tag=f"g{ex}.{oc}")
                dst, dreg = scr_reg(SG, oc)
                p.add("scalar",
                      (lambda dst=dst, lo=lo: nc.scalar.activation(
                          dst, PS[:, lo:lo + TT], AF.Silu)),
                      reads=[R("PS", lo, lo + TT)], writes=[dreg])
            for oc in range(4):
                if oc == 0:
                    load_slot(i, ("u", ex))
                lo = next_bank()
                mm_job(wslot(("u", ex), KD, 512, oc), xnf_rhs(), lo,
                       tag=f"u{ex}.{oc}")
                dst, dreg = scr_reg(UU, oc)
                p.add("scalar",
                      (lambda dst=dst, lo=lo: nc.scalar.copy(dst, PS[:, lo:lo + TT])),
                      reads=[R("PS", lo, lo + TT)], writes=[dreg])
            emit_wd(ex - 2)
            p.add("tensor",
                  (lambda ex=ex: nc.tensor.matmul(
                      PS[:, 0:512], eh[:, ex * 128:(ex + 1) * 128], rwf[0:8, :],
                      start=True, stop=True)),
                  reads=[R("eh", ex * 128, (ex + 1) * 128), R("rwf", 0, TT)],
                  writes=[R("PS", 0, 512)], tag=f"rw{ex}")
            for oc in range(4):
                sa, sreg = scr_reg(SG, oc)
                ha, hreg = scr_reg(heb, oc)
                p.add("vector",
                      (lambda sa=sa, ha=ha, sreg=sreg, hreg=hreg:
                       nc.vector.tensor_tensor(ha, sa, PS[:, 0:512], ALU.mult)),
                      reads=[sreg, R("PS", 0, 512)], writes=[hreg])
            for oc in range(4):
                ha, hreg = scr_reg(heb, oc)
                ua, ureg = scr_reg(UU, oc)
                p.add("vector",
                      (lambda ha=ha, ua=ua, hreg=hreg, ureg=ureg:
                       nc.vector.tensor_tensor(ha, ha, ua, ALU.mult)),
                      reads=[hreg, ureg], writes=[hreg])
        emit_wd(NE - 2)
        emit_wd(NE - 1)

        for oc in range(8):
            if oc % 4 == 0:
                load_slot(i, ("sg", oc // 4))
            lo = next_bank()
            mm_job(wslot(("sg", oc // 4), KD, 512, oc % 4), xnf_rhs(), lo,
                   tag=f"sg{oc}")
            p.add("scalar",
                  (lambda oc=oc, lo=lo: nc.scalar.activation(
                      scr[:, oc * TT:(oc + 1) * TT], PS[:, lo:lo + TT], AF.Silu)),
                  reads=[R("PS", lo, lo + TT)],
                  writes=[R("scr", oc * TT, (oc + 1) * TT)])
        for oc in range(8):
            if oc % 4 == 0:
                load_slot(i, ("su", oc // 4))
            lo = next_bank()
            mm_job(wslot(("su", oc // 4), KD, 512, oc % 4), xnf_rhs(), lo,
                   tag=f"su{oc}")
            p.add("vector",
                  (lambda oc=oc, lo=lo: nc.vector.tensor_tensor(
                      scr[:, 4096 + oc * TT:4096 + (oc + 1) * TT],
                      scr[:, oc * TT:(oc + 1) * TT], PS[:, lo:lo + TT], ALU.mult)),
                  reads=[R("scr", oc * TT, (oc + 1) * TT), R("PS", lo, lo + TT)],
                  writes=[R("scr", 4096 + oc * TT, 4096 + (oc + 1) * TT)])

        if i + 1 < n_tiles:
            ln1_apply(i + 1)

        prod_rhs = [(scr[:, 4096 + c * TT:4096 + (c + 1) * TT],
                     R("scr", 4096 + c * TT, 4096 + (c + 1) * TT))
                    for c in range(KD)]
        for oc in range(8):
            if oc % 4 == 0:
                load_slot(i, ("sd", oc // 4))
            lo = next_bank()
            mm_job(wslot(("sd", oc // 4), KD, 512, oc % 4), prod_rhs, lo,
                   tag=f"sd{oc}")
            p.add("vector",
                  (lambda oc=oc, lo=lo: nc.vector.tensor_tensor(
                      yacc[:, oc * TT:(oc + 1) * TT],
                      yacc[:, oc * TT:(oc + 1) * TT],
                      PS[:, lo:lo + TT], ALU.add)),
                  reads=[R("yacc", oc * TT, (oc + 1) * TT), R("PS", lo, lo + TT)],
                  writes=[R("yacc", oc * TT, (oc + 1) * TT)])
            p.add("scalar",
                  (lambda oc=oc: nc.scalar.copy(
                      scr[:, oc * TT:(oc + 1) * TT], yacc[:, oc * TT:(oc + 1) * TT])),
                  reads=[R("yacc", oc * TT, (oc + 1) * TT)],
                  writes=[R("scr", oc * TT, (oc + 1) * TT)])

        def make_yfin(i=i):
            par = i % 2
            for s in range(4):
                half = s % 2
                for c in range(8):
                    p.add("tensor",
                          (lambda s=s, c=c, half=half: nc.tensor.transpose(
                              PT[:, half * 1024 + c * 128: half * 1024 + (c + 1) * 128],
                              scr[:, c * TT + s * 128: c * TT + (s + 1) * 128],
                              iden[:, :])),
                          reads=[R("scr", c * TT + s * 128, c * TT + (s + 1) * 128),
                                 R("iden", 0, 128)],
                          writes=[R("PT", half * 1024 + c * 128,
                                    half * 1024 + (c + 1) * 128)])
                p.add("vector",
                      (lambda i=i, s=s, half=half: nc.vector.tensor_tensor(
                          xsub(i, s), xsub(i, s),
                          PT[:, half * 1024:(half + 1) * 1024], ALU.add)),
                      reads=[xreg(i, s), R("PT", half * 1024, (half + 1) * 1024)],
                      writes=[xreg(i, s)])
            dst = y_h[i * TT:(i + 1) * TT, :].rearrange("(s p) d -> p s d", p=128)
            ysrc = xtok[:, par * 4096:(par + 1) * 4096].rearrange(
                "p (s d) -> p s d", d=D)
            p.add("gpsimd",
                  (lambda dst=dst, ysrc=ysrc: nc.gpsimd.dma_start(dst, ysrc)),
                  reads=[R("xtok", par * 4096, (par + 1) * 4096)],
                  writes=[R("y_h", i * TT * D, (i + 1) * TT * D)],
                  dma="q_gp", tag=f"y{i}")
        pend_yfin[0] = make_yfin

    if pend_yfin[0] is not None:
        pend_yfin[0]()
        pend_yfin[0] = None
    counts = dict(p.dma_counts)
    p.add("gpsimd", lambda: nc.gpsimd.nop(), reads=[R("y_h", 0, T * D)],
          tag="final")
    p.finalize()

    per_engine = {en: [op for op in p.ops if op["eng"] == en] for en in ENGINES}

    with ctx:
        with nc.Block() as block:
            def emit_stream(en):
                eng_obj = getattr(nc, en)
                for op in per_engine[en]:
                    for sem_name, val in op["waits"]:
                        eng_obj.wait_ge(sems[sem_name], val)
                    inst = op["emit"]()
                    if op["dma"] is not None:
                        inst.then_inc(sems[op["dma"]], 16)
                    elif op["inc"]:
                        inst.then_inc(sems["e_" + en], 1)

            @block.tensor
            def _(t):
                emit_stream("tensor")

            @block.vector
            def _(v):
                emit_stream("vector")

            @block.scalar
            def _(s):
                emit_stream("scalar")

            @block.sync
            def _(sy):
                emit_stream("sync")
                for ln in range(NLANE):
                    sy.wait_ge(sems[f"q_sync{ln}"], counts.get(f"q_sync{ln}", 0))

            @block.gpsimd
            def _(g):
                emit_stream("gpsimd")
                for ln in range(NLANE):
                    g.wait_ge(sems[f"q_gp{ln}"], counts.get(f"q_gp{ln}", 0))
    return nc



def _img(Wm, kchunks, sl=None):
    d_in, d_out = Wm.shape
    assert d_in == kchunks * 128
    V = Wm.reshape(kchunks, 128, d_out)
    if sl is not None:
        V = V[:, :, sl]
    return np.ascontiguousarray(V.transpose(1, 0, 2).reshape(128, -1))


def _prep(weights):
    (ipw, convw, convb, lng, lnb, fcw, fcb, cpw, cpb, oww,
     wr, wg, wu, wd, sg, su, sd, eb) = weights
    imgs = {}
    ipT = np.ascontiguousarray(ipw.T)
    for j in range(8):
        imgs[("ip", j)] = _img(ipT, 8, slice(j * 512, (j + 1) * 512))
    for j in range(4):
        img = np.empty((128, 4096), np.float32)
        for cc in range(2):
            for k in range(4):
                for og in range(2):
                    for oc2 in range(2):
                        col = cc * 2048 + k * 512 + og * 256 + oc2 * 128
                        blk = convw[(2 * j + og) * 256 + oc2 * 128:
                                    (2 * j + og) * 256 + (oc2 + 1) * 128,
                                    cc * 128:(cc + 1) * 128, k]
                        img[:, col:col + 128] = blk.T
        imgs[("cv", j)] = img
    fcT = np.ascontiguousarray(fcw.T)
    for j in range(4):
        imgs[("fc", j)] = _img(fcT, 16, slice(j * 256, (j + 1) * 256))
    cpT = np.ascontiguousarray(cpw.T)
    for j in range(4):
        imgs[("cp", j)] = _img(cpT, 8, slice(j * 512, (j + 1) * 512))
    owT = np.ascontiguousarray(oww.T)
    for j in range(4):
        imgs[("ow", j)] = _img(owT, 16, slice(j * 256, (j + 1) * 256))
    for ex in range(8):
        imgs[("g", ex)] = _img(wg[ex], 8)
        imgs[("u", ex)] = _img(wu[ex], 8)
        imgs[("d", ex)] = _img(wd[ex], 4)
    for j in range(2):
        imgs[("sg", j)] = _img(sg, 8, slice(j * 512, (j + 1) * 512))
        imgs[("su", j)] = _img(su, 8, slice(j * 512, (j + 1) * 512))
        imgs[("sd", j)] = _img(sd, 8, slice(j * 512, (j + 1) * 512))
    wimg = np.stack([imgs[k] for k in SLOT_ORDER], axis=0)
    assert wimg.shape == (NSLOT, 128, SLOTC), wimg.shape

    cvecs = np.zeros((128, 80), np.float32)
    cvecs[:, 0:16] = convb.reshape(16, 128).T
    cvecs[:, 16:32] = lng.reshape(16, 128).T
    cvecs[:, 32:48] = lnb.reshape(16, 128).T
    cvecs[:, 48:56] = fcb.reshape(8, 128).T
    cvecs[:, 56:72] = cpb.reshape(16, 128).T
    cvecs[0:8, 72] = eb
    wrimg = np.zeros((128, 64), np.float32)
    for c in range(8):
        wrimg[:, c * 8:(c + 1) * 8] = wr[c * 128:(c + 1) * 128, :]
    return wimg, cvecs, wrimg


def kernel(L0_attn_hidden_states, L0_attn_in_proj_w, L0_attn_conv_w,
           L0_attn_conv_b, L0_attn_ln_g, L0_attn_ln_b, L0_attn_fc_w,
           L0_attn_fc_b, L0_attn_cproj_w, L0_attn_cproj_b, L0_attn_out_w,
           L0_ffn_Wr, L0_ffn_Wg, L0_ffn_Wu, L0_ffn_Wd, L0_ffn_Sg, L0_ffn_Su,
           L0_ffn_Sd, L0_ffn_expert_bias):
    x = np.ascontiguousarray(np.asarray(L0_attn_hidden_states, dtype=np.float32))
    B, T, _ = x.shape
    weights = [np.ascontiguousarray(np.asarray(w_, dtype=np.float32)) for w_ in (
        L0_attn_in_proj_w, L0_attn_conv_w, L0_attn_conv_b, L0_attn_ln_g,
        L0_attn_ln_b, L0_attn_fc_w, L0_attn_fc_b, L0_attn_cproj_w,
        L0_attn_cproj_b, L0_attn_out_w, L0_ffn_Wr, L0_ffn_Wg, L0_ffn_Wu,
        L0_ffn_Wd, L0_ffn_Sg, L0_ffn_Su, L0_ffn_Sd, L0_ffn_expert_bias)]
    wimg, cvecs, wrimg = _prep(weights)
    nc = build_nc(T // TT)
    in_maps = [{"x": x[c], "wimg": wimg, "cvec": cvecs, "wrimg": wrimg}
               for c in range(B)]
    res = run_bass_kernel_spmd(nc, in_maps, core_ids=list(range(B)))
    return np.stack([r["y"] for r in res.results], axis=0)


# version 11
# speedup vs baseline: 1.1370x; 1.1370x over previous
import numpy as np
from contextlib import ExitStack
import concourse.bass as bass
import concourse.mybir as mybir
from concourse.bass_utils import run_bass_kernel_spmd

F32 = mybir.dt.float32
BF16 = mybir.dt.bfloat16
AF = mybir.ActivationFunctionType
ALU = mybir.AluOpType
AX = mybir.AxisListType

D = 1024
I = 2048
TT = 512
KD = D // 128
KI = I // 128
NE = 8
NSLOT = 54
SLOTC = 4096
RING = 6
W = TT + 3
NLANE = 4

ENGINES = ("tensor", "vector", "scalar", "gpsimd", "sync")


def slot_order():
    order = [("ip", j) for j in range(8)]
    order += [("cv", j) for j in range(4)]
    order += [("g", 0), ("u", 0), ("g", 1), ("u", 1)]
    order += [("fc", j) for j in range(4)]
    order += [("cp", j) for j in range(4)]
    order += [("ow", j) for j in range(4)]
    for ex in range(2, NE):
        order += [("g", ex), ("u", ex), ("d", ex - 2)]
    order += [("d", 6), ("d", 7)]
    order += [("sg", 0), ("sg", 1), ("su", 0), ("su", 1), ("sd", 0), ("sd", 1)]
    return order


SLOT_ORDER = slot_order()
SLOT = {k: idx for idx, k in enumerate(SLOT_ORDER)}
assert len(SLOT_ORDER) == NSLOT


class Plan:

    def __init__(self):
        self.ops = []
        self.writes = {}
        self.reads = {}
        self.dma_counts = {}
        self.dma_rr = {}

    @staticmethod
    def _overlap(recs, lo, hi):
        return [r for r in recs if r[0] < hi and lo < r[1]]

    def add(self, eng, emit, reads=(), writes=(), dma=None, tag=""):
        opid = len(self.ops)
        deps = set()
        for key, lo, hi in reads:
            for _, _, w in self._overlap(self.writes.get(key, ()), lo, hi):
                deps.add(w)
        for key, lo, hi in writes:
            for _, _, w in self._overlap(self.writes.get(key, ()), lo, hi):
                deps.add(w)
            for _, _, r in self._overlap(self.reads.get(key, ()), lo, hi):
                deps.add(r)
        dma_val = None
        dma_gate = None
        if dma is not None:
            lane = self.dma_rr.get(dma, 0)
            self.dma_rr[dma] = (lane + 1) % NLANE
            dma = f"{dma}{lane}"
            prev = self.dma_counts.get(dma, 0)
            if prev > 0:
                dma_gate = (dma, prev)
            self.dma_counts[dma] = prev + 16
            dma_val = self.dma_counts[dma]
        self.ops.append(dict(id=opid, eng=eng, emit=emit, deps=deps, dma=dma,
                             dma_val=dma_val, dma_gate=dma_gate, tag=tag))
        for key, lo, hi in writes:
            wr = self.writes.setdefault(key, [])
            wr[:] = [r for r in wr if not (lo <= r[0] and r[1] <= hi)]
            wr.append((lo, hi, opid))
            rd = self.reads.setdefault(key, [])
            rd[:] = [r for r in rd if not (lo <= r[0] and r[1] <= hi)]
        for key, lo, hi in reads:
            self.reads.setdefault(key, []).append((lo, hi, opid))
        return opid

    def finalize(self):
        ops = self.ops
        for op in ops:
            best = {}
            for d in op["deps"]:
                dop = ops[d]
                if (dop["eng"] == "tensor" and op["eng"] == "tensor"
                        and dop["dma"] is None and op["dma"] is None):
                    continue
                key = (dop["eng"], dop["dma"])
                if key not in best or d > best[key]:
                    best[key] = d
            op["deps"] = set(best.values())
        targets = set()
        for op in ops:
            for d in op["deps"]:
                if ops[d]["dma"] is None:
                    targets.add(d)
        rank = {}
        counts = {e: 0 for e in ENGINES}
        for op in ops:
            if op["dma"] is None and op["id"] in targets:
                counts[op["eng"]] += 1
                rank[op["id"]] = counts[op["eng"]]
                op["inc"] = True
            else:
                op["inc"] = op["dma"] is not None
        highwater = {e: {} for e in ENGINES}
        for op in ops:
            waits = []
            cands = []
            for d in sorted(op["deps"]):
                dop = ops[d]
                if dop["dma"] is not None:
                    cands.append((dop["dma"], dop["dma_val"]))
                else:
                    cands.append(("e_" + dop["eng"], rank[d]))
            if op["dma_gate"] is not None:
                cands.append(op["dma_gate"])
            for sem, val in cands:
                hw = highwater[op["eng"]]
                if hw.get(sem, 0) < val:
                    hw[sem] = val
                    waits.append((sem, val))
            op["waits"] = waits
        return counts


def build_nc(n_tiles):
    T = n_tiles * TT
    p = Plan()
    nc = bass.Bass("TRN2", target_bir_lowering=False)

    x_h = nc.dram_tensor("x", [T, D], F32, kind="ExternalInput").ap()
    wimg_h = nc.dram_tensor("wimg", [NSLOT, 128, SLOTC], F32, kind="ExternalInput").ap()
    cvec_h = nc.dram_tensor("cvec", [128, 80], F32, kind="ExternalInput").ap()
    wr_h = nc.dram_tensor("wrimg", [128, 64], F32, kind="ExternalInput").ap()
    y_h = nc.dram_tensor("y", [T, D], F32, kind="ExternalOutput").ap()
    wsc_h = nc.dram_tensor("wsc", [NSLOT, 128, SLOTC], BF16, kind="Internal").ap()

    ctx = ExitStack()
    e = ctx.enter_context
    xtok = e(nc.sbuf_tensor("xtok", [128, 2 * 4096], F32))
    xntok = e(nc.sbuf_tensor("xntok", [128, 4096], BF16))
    xnf = e(nc.sbuf_tensor("xnf", [128, 4096], BF16))
    hpre = e(nc.sbuf_tensor("hpre", [128, KI * W], BF16))
    gate = e(nc.sbuf_tensor("gate", [128, 8192], BF16))
    hbuf = e(nc.sbuf_tensor("hbuf", [128, 8192], BF16))
    tfc = e(nc.sbuf_tensor("tfc", [128, 4096], BF16))
    scr = e(nc.sbuf_tensor("scr", [128, 8192], BF16))
    yacc = e(nc.sbuf_tensor("yacc", [128, 4096], F32))
    sqscr = e(nc.sbuf_tensor("sqscr", [128, 2048], BF16))
    tmp = e(nc.sbuf_tensor("tmp", [128, 1024], F32))
    ring = e(nc.sbuf_tensor("ring", [128, RING * SLOTC], BF16))
    cvec = e(nc.sbuf_tensor("cvec_sb", [128, 80], F32))
    wrsb = e(nc.sbuf_tensor("wrsb", [128, 64], BF16))
    ones = e(nc.sbuf_tensor("ones_sb", [128, 2], BF16))
    onesrow = e(nc.sbuf_tensor("onesrow", [1, 128], BF16))
    iden = e(nc.sbuf_tensor("iden", [128, 128], BF16))
    iden32 = e(nc.sbuf_tensor("iden32", [128, 128], F32))
    eh = e(nc.sbuf_tensor("eh", [8, 1024], BF16))
    rows = e(nc.sbuf_tensor("rows", [1, 1536], F32))
    abbf = e(nc.sbuf_tensor("abbf", [1, 1024], BF16))
    lgsb = e(nc.sbuf_tensor("lgsb", [8, TT], F32))
    rtok = e(nc.sbuf_tensor("rtok", [128, 32], F32))
    msk = e(nc.sbuf_tensor("msk", [128, 64], F32))
    rwtok = e(nc.sbuf_tensor("rwtok", [128, 32], F32))
    rwf = e(nc.sbuf_tensor("rwf", [8, TT], BF16))
    rsc = e(nc.sbuf_tensor("rsc", [128, 24], F32))
    mln1 = e(nc.sbuf_tensor("mln1", [128, 4], F32))
    rln1 = e(nc.sbuf_tensor("rln1", [128, 4], F32))
    sln1 = e(nc.sbuf_tensor("sln1", [128, 4], F32))
    qln1 = e(nc.sbuf_tensor("qln1", [128, 4], F32))
    vsc = e(nc.sbuf_tensor("vsc", [128, 8], F32))
    consts = e(nc.sbuf_tensor("consts", [128, 4], F32))
    fbias = e(nc.sbuf_tensor("fbias", [128, 8], F32))
    lnbbf = e(nc.sbuf_tensor("lnbbf", [128, 16], BF16))
    PS = e(nc.psum_tensor("PS", [128, 3072], F32))
    PT = e(nc.psum_tensor("PT", [128, 2048], BF16))

    nc.const_aps.register(F32, 1e-5, consts[:, 0:1])
    nc.const_aps.register(F32, -0.5, consts[:, 1:2])
    nc.const_aps.register(F32, -1.0, consts[:, 2:3])

    sems = {}
    sem_names = ["e_tensor", "e_vector", "e_scalar", "e_gpsimd", "e_sync"]
    sem_names += [f"q_sync{i}" for i in range(NLANE)]
    sem_names += [f"q_gp{i}" for i in range(NLANE)]
    for name in sem_names:
        sems[name] = e(nc.semaphore(name))

    def R(t, lo, hi):
        return (t, lo, hi)

    rot = {"banks": [1024, 1536, 2048, 2560], "i": 0}

    def next_bank():
        b = rot["banks"][rot["i"] % len(rot["banks"])]
        rot["i"] += 1
        return b

    def mm_job(lhsT, rhs, out_lo, m_hi=128, tag=""):
        n = len(lhsT)
        for k in range(n):
            st, sp = (k == 0), (k == n - 1)
            la, lreg = lhsT[k]
            ra, rreg = rhs[k]
            p.add("tensor",
                  (lambda la=la, ra=ra, st=st, sp=sp, lo=out_lo, mh=m_hi:
                   nc.tensor.matmul(PS[0:mh, lo:lo + TT], la, ra, start=st, stop=sp)),
                  reads=[lreg, rreg], writes=[R("PS", out_lo, out_lo + TT)], tag=tag)

    def slot_base(s):
        return (s % RING) * SLOTC

    def load_slot(tile, key):
        s = SLOT[key]
        b = slot_base(s)
        if tile == 0:
            p.add("gpsimd",
                  (lambda s=s, b=b: nc.gpsimd.dma_start(
                      ring[:, b:b + SLOTC], wimg_h[s])),
                  reads=[R("wimg", s * SLOTC, (s + 1) * SLOTC)],
                  writes=[R("ring", b, b + SLOTC)], dma="q_gp", tag=f"cast{s}")
            p.add("sync",
                  (lambda s=s, b=b: nc.sync.dma_start(
                      wsc_h[s], ring[:, b:b + SLOTC])),
                  reads=[R("ring", b, b + SLOTC)],
                  writes=[R("wsc", s * SLOTC, (s + 1) * SLOTC)],
                  dma="q_sync", tag=f"wb{s}")
        else:
            p.add("sync",
                  (lambda s=s, b=b: nc.sync.dma_start(
                      ring[:, b:b + SLOTC], wsc_h[s])),
                  reads=[R("wsc", s * SLOTC, (s + 1) * SLOTC)],
                  writes=[R("ring", b, b + SLOTC)], dma="q_sync", tag=f"ld{s}")

    def slot_ap(key, lo, hi):
        b = slot_base(SLOT[key])
        return (ring[:, b + lo:b + hi], R("ring", b + lo, b + hi))

    def wslot(key, kchunks, ow, jj):
        return [slot_ap(key, c * ow + jj * 128, c * ow + (jj + 1) * 128)
                for c in range(kchunks)]

    def xnf_rhs():
        return [(xnf[:, c * TT:(c + 1) * TT], R("xnf", c * TT, (c + 1) * TT))
                for c in range(KD)]

    SG, UU, HE0 = 0, 2048, 4096

    def scr_reg(base, oc):
        return (scr[:, base + oc * TT:base + (oc + 1) * TT],
                R("scr", base + oc * TT, base + (oc + 1) * TT))

    p.add("sync", lambda: nc.sync.dma_start(cvec[:, :], cvec_h),
          writes=[R("cvec_sb", 0, 80)], dma="q_sync")
    p.add("gpsimd", lambda: nc.gpsimd.dma_start(wrsb[:, :], wr_h),
          writes=[R("wrsb", 0, 64)], dma="q_gp")
    p.add("gpsimd", lambda: nc.gpsimd.memset(ones[:, :], 1.0),
          writes=[R("ones_sb", 0, 2)])
    p.add("gpsimd", lambda: nc.gpsimd.memset(onesrow[:, :], 1.0),
          writes=[R("onesrow", 0, 128)])
    p.add("gpsimd", lambda: nc.gpsimd.memset(consts[:, 0:1], 1e-5),
          writes=[R("consts", 0, 1)])
    p.add("gpsimd", lambda: nc.gpsimd.memset(consts[:, 1:2], -0.5),
          writes=[R("consts", 1, 2)])
    p.add("gpsimd", lambda: nc.gpsimd.memset(consts[:, 2:3], -1.0),
          writes=[R("consts", 2, 3)])
    p.add("gpsimd", lambda: nc.gpsimd.iota(tmp[:, 0:128], [[0, 128]],
          channel_multiplier=1, allow_small_or_imprecise_dtypes=True),
          writes=[R("tmp", 0, 128)])
    p.add("gpsimd", lambda: nc.gpsimd.iota(tmp[:, 512:640], [[1, 128]],
          channel_multiplier=0, allow_small_or_imprecise_dtypes=True),
          writes=[R("tmp", 512, 640)])
    p.add("vector", lambda: nc.vector.tensor_tensor(
        iden[:, :], tmp[:, 0:128], tmp[:, 512:640], ALU.is_equal),
        reads=[R("tmp", 0, 128), R("tmp", 512, 640)], writes=[R("iden", 0, 128)])
    p.add("vector", lambda: nc.vector.tensor_tensor(
        iden32[:, :], tmp[:, 0:128], tmp[:, 512:640], ALU.is_equal),
        reads=[R("tmp", 0, 128), R("tmp", 512, 640)], writes=[R("iden32", 0, 128)])
    p.add("gpsimd", lambda: nc.gpsimd.iota(tmp[0:8, 0:1024], [[1, 8], [0, 128]],
          channel_multiplier=0, allow_small_or_imprecise_dtypes=True),
          writes=[R("tmp", 0, 1024)])
    p.add("gpsimd", lambda: nc.gpsimd.iota(yacc[0:8, 0:1024], [[0, 8], [0, 128]],
          channel_multiplier=1, allow_small_or_imprecise_dtypes=True),
          writes=[R("yacc", 0, 1024)])
    p.add("vector", lambda: nc.vector.tensor_tensor(
        eh[:, :], tmp[0:8, 0:1024], yacc[0:8, 0:1024], ALU.is_equal),
        reads=[R("tmp", 0, 1024), R("yacc", 0, 1024)], writes=[R("eh", 0, 1024)])
    p.add("gpsimd", lambda: nc.gpsimd.memset(
        hpre[:, :].rearrange("p (c w) -> p c w", w=W)[:, :, 0:3], 0.0),
        writes=[R("hpre", 0, KI * W)])

    def load_x(i):
        par = i % 2
        src = x_h[i * TT:(i + 1) * TT, :].rearrange("(s p) d -> p s d", p=128)
        dst = xtok[:, par * 4096:(par + 1) * 4096].rearrange(
            "p (s d) -> p s d", d=D)
        p.add("sync",
              (lambda src=src, dst=dst: nc.sync.dma_start(dst, src)),
              writes=[R("xtok", par * 4096, (par + 1) * 4096)],
              dma="q_sync", tag=f"x{i}")

    def xreg(i, s):
        par = i % 2
        return R("xtok", par * 4096 + s * 1024, par * 4096 + (s + 1) * 1024)

    def xsub(i, s):
        par = i % 2
        return xtok[:, par * 4096 + s * 1024: par * 4096 + (s + 1) * 1024]

    def ln1_stats(i):
        for s in range(4):
            p.add("vector",
                  (lambda i=i, s=s: nc.vector.reduce_sum(
                      sln1[:, s:s + 1], xsub(i, s), AX.X)),
                  reads=[xreg(i, s)], writes=[R("sln1", s, s + 1)])
            p.add("scalar",
                  (lambda i=i, s=s: nc.scalar.activation(
                      sqscr[:, 0:1024], xsub(i, s), AF.Square,
                      accum_out=qln1[:, s:s + 1])),
                  reads=[xreg(i, s)],
                  writes=[R("sqscr", 0, 1024), R("qln1", s, s + 1)])
        p.add("vector", lambda: nc.vector.tensor_scalar_mul(
            mln1[:, 0:4], sln1[:, 0:4], 1.0 / D),
            reads=[R("sln1", 0, 4)], writes=[R("mln1", 0, 4)])
        p.add("vector", lambda: nc.vector.tensor_tensor(
            vsc[:, 0:4], mln1[:, 0:4], mln1[:, 0:4], ALU.mult),
            reads=[R("mln1", 0, 4)], writes=[R("vsc", 0, 4)])
        p.add("vector", lambda: nc.vector.scalar_tensor_tensor(
            vsc[:, 4:8], qln1[:, 0:4], 1.0 / D, vsc[:, 0:4],
            ALU.mult, ALU.subtract),
            reads=[R("qln1", 0, 4), R("vsc", 0, 4)], writes=[R("vsc", 4, 8)])
        p.add("scalar", lambda: nc.scalar.activation(
            rln1[:, 0:4], vsc[:, 4:8], AF.Ln, bias=1e-5),
            reads=[R("vsc", 4, 8), R("consts", 0, 1)], writes=[R("rln1", 0, 4)])
        p.add("scalar", lambda: nc.scalar.activation(
            rln1[:, 0:4], rln1[:, 0:4], AF.Exp, scale=-0.5),
            reads=[R("rln1", 0, 4), R("consts", 1, 2)], writes=[R("rln1", 0, 4)])

    def ln1_apply_a(i):
        for s in range(4):
            p.add("vector",
                  (lambda i=i, s=s: nc.vector.tensor_scalar(
                      xntok[:, s * 1024:(s + 1) * 1024], xsub(i, s),
                      mln1[:, s:s + 1], rln1[:, s:s + 1],
                      ALU.subtract, ALU.mult)),
                  reads=[xreg(i, s), R("mln1", s, s + 1), R("rln1", s, s + 1)],
                  writes=[R("xntok", s * 1024, (s + 1) * 1024)])

    def ln1_apply(i):
        for s in range(4):
            half = s % 2
            for c in range(8):
                p.add("tensor",
                      (lambda s=s, c=c, half=half: nc.tensor.transpose(
                          PT[:, half * 1024 + c * 128: half * 1024 + (c + 1) * 128],
                          xntok[:, s * 1024 + c * 128: s * 1024 + (c + 1) * 128],
                          iden[:, :])),
                      reads=[R("xntok", s * 1024 + c * 128, s * 1024 + (c + 1) * 128),
                             R("iden", 0, 128)],
                      writes=[R("PT", half * 1024 + c * 128,
                                half * 1024 + (c + 1) * 128)])
            out_ap = xnf[:, :].rearrange("p (c sb t) -> p c sb t",
                                         c=8, sb=4, t=128)[:, :, s, :]
            in_ap = PT[:, half * 1024:(half + 1) * 1024].rearrange(
                "p (c t) -> p c t", t=128)
            p.add("scalar",
                  (lambda out_ap=out_ap, in_ap=in_ap:
                   nc.scalar.copy(out_ap, in_ap)),
                  reads=[R("PT", half * 1024, (half + 1) * 1024)],
                  writes=[R("xnf", 0, 4096)])

    load_x(0)
    ln1_stats(0)
    ln1_apply_a(0)
    ln1_apply(0)

    pend_yfin = [None]
    for i in range(n_tiles):
        rot["banks"] = [1024, 1536, 2048, 2560]

        for oc in range(32):
            if oc % 4 == 0:
                load_slot(i, ("ip", oc // 4))
            lo = next_bank()
            mm_job(wslot(("ip", oc // 4), KD, 512, oc % 4), xnf_rhs(), lo,
                   tag=f"ip{oc}")
            if oc < 16:
                p.add("scalar",
                      (lambda oc=oc, lo=lo: nc.scalar.copy(
                          hpre[:, oc * W + 3: oc * W + 3 + TT], PS[:, lo:lo + TT])),
                      reads=[R("PS", lo, lo + TT)],
                      writes=[R("hpre", oc * W + 3, oc * W + 3 + TT)])
            else:
                g = oc - 16
                p.add("scalar",
                      (lambda g=g, lo=lo: nc.scalar.activation(
                          gate[:, g * TT:(g + 1) * TT], PS[:, lo:lo + TT], AF.Silu)),
                      reads=[R("PS", lo, lo + TT)],
                      writes=[R("gate", g * TT, (g + 1) * TT)])

        if pend_yfin[0] is not None:
            pend_yfin[0]()
            pend_yfin[0] = None

        def stat_chunk(c):
            sq = c % 4
            p.add("scalar",
                  (lambda c=c, sq=sq: nc.scalar.activation(
                      sqscr[:, sq * TT:(sq + 1) * TT],
                      hbuf[:, c * TT:(c + 1) * TT], AF.Square)),
                  reads=[R("hbuf", c * TT, (c + 1) * TT)],
                  writes=[R("sqscr", sq * TT, (sq + 1) * TT)])
            p.add("tensor",
                  (lambda c=c: nc.tensor.matmul(
                      PS[0:1, 0:512], ones[:, 0:1], hbuf[:, c * TT:(c + 1) * TT],
                      start=(c == 0), stop=(c == KI - 1))),
                  reads=[R("ones_sb", 0, 1), R("hbuf", c * TT, (c + 1) * TT)],
                  writes=[R("PS", 0, 512)], tag="S1")
            p.add("tensor",
                  (lambda c=c, sq=sq: nc.tensor.matmul(
                      PS[0:1, 512:1024], ones[:, 0:1],
                      sqscr[:, sq * TT:(sq + 1) * TT],
                      start=(c == 0), stop=(c == KI - 1))),
                  reads=[R("ones_sb", 0, 1), R("sqscr", sq * TT, (sq + 1) * TT)],
                  writes=[R("PS", 512, 1024)], tag="S2")

        for oc in range(KI):
            if oc % 4 == 0:
                load_slot(i, ("cv", oc // 4))
            g = oc // 2
            oc2 = oc % 2
            og = g % 2
            lh, rh = [], []
            for cc in range(2):
                gc = 2 * g + cc
                for k in range(4):
                    lh.append(slot_ap(("cv", g // 2),
                                      cc * 2048 + k * 512 + og * 256 + oc2 * 128,
                                      cc * 2048 + k * 512 + og * 256 + (oc2 + 1) * 128))
                    rh.append((hpre[:, gc * W + k: gc * W + k + TT],
                               R("hpre", gc * W + k, gc * W + k + TT)))
            lo = next_bank()
            mm_job(lh, rh, lo, tag=f"cv{oc}")
            p.add("vector",
                  (lambda oc=oc, lo=lo: nc.vector.scalar_tensor_tensor(
                      hbuf[:, oc * TT:(oc + 1) * TT], PS[:, lo:lo + TT],
                      cvec[:, oc:oc + 1], gate[:, oc * TT:(oc + 1) * TT],
                      ALU.add, ALU.mult)),
                  reads=[R("PS", lo, lo + TT), R("cvec_sb", oc, oc + 1),
                         R("gate", oc * TT, (oc + 1) * TT)],
                  writes=[R("hbuf", oc * TT, (oc + 1) * TT)])
            if oc >= 3:
                stat_chunk(oc - 3)
        for c_ in range(KI - 3, KI):
            stat_chunk(c_)

        if i + 1 < n_tiles:
            load_x(i + 1)
            src = hpre[:, :].rearrange("p (c w) -> p c w", w=W)[:, :, TT:TT + 3]
            dst = hpre[:, :].rearrange("p (c w) -> p c w", w=W)[:, :, 0:3]
            p.add("vector",
                  (lambda src=src, dst=dst: nc.vector.tensor_copy(dst, src)),
                  reads=[R("hpre", 0, KI * W)], writes=[R("hpre", 0, KI * W)],
                  tag="halo")

        lo_lg = next_bank()
        mm_job([(wrsb[:, c * 8:(c + 1) * 8], R("wrsb", c * 8, (c + 1) * 8))
                for c in range(KD)], xnf_rhs(), lo_lg, m_hi=8, tag="lg")
        p.add("vector",
              (lambda lo=lo_lg: nc.vector.tensor_scalar_add(
                  lgsb[:, :], PS[0:8, lo:lo + TT], cvec[0:8, 72:73])),
              reads=[R("PS", lo_lg, lo_lg + TT), R("cvec_sb", 72, 73)],
              writes=[R("lgsb", 0, TT)])
        lo_t = next_bank()
        for s in range(4):
            p.add("tensor",
                  (lambda s=s, lo=lo_t: nc.tensor.transpose(
                      PS[:, lo + s * 8:lo + (s + 1) * 8],
                      lgsb[0:8, s * 128:(s + 1) * 128], iden32[0:8, 0:8])),
                  reads=[R("lgsb", s * 128, (s + 1) * 128), R("iden32", 0, 8)],
                  writes=[R("PS", lo_t + s * 8, lo_t + (s + 1) * 8)])
        p.add("vector",
              (lambda lo=lo_t: nc.vector.tensor_copy(rtok[:, :], PS[:, lo:lo + 32])),
              reads=[R("PS", lo_t, lo_t + 32)], writes=[R("rtok", 0, 32)])

        p.add("vector", lambda: nc.vector.tensor_scalar_mul(
            rows[:, 0:512], PS[0:1, 0:512], 1.0 / I),
            reads=[R("PS", 0, 512)], writes=[R("rows", 0, 512)])
        p.add("vector", lambda: nc.vector.tensor_tensor(
            rows[:, 1024:1536], rows[:, 0:512], rows[:, 0:512], ALU.mult),
            reads=[R("rows", 0, 512)], writes=[R("rows", 1024, 1536)])
        p.add("vector", lambda: nc.vector.scalar_tensor_tensor(
            rows[:, 512:1024], PS[0:1, 512:1024], 1.0 / I, rows[:, 1024:1536],
            ALU.mult, ALU.subtract),
            reads=[R("PS", 512, 1024), R("rows", 1024, 1536)],
            writes=[R("rows", 512, 1024)])
        p.add("scalar", lambda: nc.scalar.activation(
            rows[:, 1024:1536], rows[:, 512:1024], AF.Ln, bias=1e-5),
            reads=[R("rows", 512, 1024), R("consts", 0, 1)],
            writes=[R("rows", 1024, 1536)])
        p.add("scalar", lambda: nc.scalar.activation(
            rows[:, 512:1024], rows[:, 1024:1536], AF.Exp, scale=-0.5),
            reads=[R("rows", 1024, 1536), R("consts", 1, 2)],
            writes=[R("rows", 512, 1024)])
        p.add("scalar", lambda: nc.scalar.copy(abbf[:, 0:512], rows[:, 512:1024]),
              reads=[R("rows", 512, 1024)], writes=[R("abbf", 0, 512)])
        p.add("vector", lambda: nc.vector.scalar_tensor_tensor(
            abbf[:, 512:1024], rows[:, 0:512], -1.0, rows[:, 512:1024],
            ALU.mult, ALU.mult),
            reads=[R("rows", 0, 1024)], writes=[R("abbf", 512, 1024)])

        for oc in range(4):
            if oc == 0:
                load_slot(i, ("g", 0))
            lo = next_bank()
            mm_job(wslot(("g", 0), KD, 512, oc), xnf_rhs(), lo, tag=f"g0.{oc}")
            dst, dreg = scr_reg(HE0, oc)
            p.add("scalar",
                  (lambda dst=dst, lo=lo: nc.scalar.activation(
                      dst, PS[:, lo:lo + TT], AF.Silu)),
                  reads=[R("PS", lo, lo + TT)], writes=[dreg])

        p.add("tensor", lambda: nc.tensor.matmul(
            PS[:, 0:512], onesrow[0:1, :], abbf[0:1, 0:512], start=True, stop=True),
            reads=[R("onesrow", 0, 128), R("abbf", 0, 512)],
            writes=[R("PS", 0, 512)], tag="bcA")
        p.add("tensor", lambda: nc.tensor.matmul(
            PS[:, 512:1024], onesrow[0:1, :], abbf[0:1, 512:1024],
            start=True, stop=True),
            reads=[R("onesrow", 0, 128), R("abbf", 512, 1024)],
            writes=[R("PS", 512, 1024)], tag="bcB")
        for oc in range(4):
            if oc == 0:
                load_slot(i, ("u", 0))
            lo = next_bank()
            mm_job(wslot(("u", 0), KD, 512, oc), xnf_rhs(), lo, tag=f"u0.{oc}")
            dst, dreg = scr_reg(SG, oc)
            p.add("scalar",
                  (lambda dst=dst, lo=lo: nc.scalar.copy(dst, PS[:, lo:lo + TT])),
                  reads=[R("PS", lo, lo + TT)], writes=[dreg])
        for oc in range(4):
            if oc == 0:
                load_slot(i, ("g", 1))
            lo = next_bank()
            mm_job(wslot(("g", 1), KD, 512, oc), xnf_rhs(), lo, tag=f"g1.{oc}")
            dst, dreg = scr_reg(HE0 + 2048, oc)
            p.add("scalar",
                  (lambda dst=dst, lo=lo: nc.scalar.activation(
                      dst, PS[:, lo:lo + TT], AF.Silu)),
                  reads=[R("PS", lo, lo + TT)], writes=[dreg])
        for oc in range(4):
            if oc == 0:
                load_slot(i, ("u", 1))
            lo = next_bank()
            mm_job(wslot(("u", 1), KD, 512, oc), xnf_rhs(), lo, tag=f"u1.{oc}")
            dst, dreg = scr_reg(UU, oc)
            p.add("scalar",
                  (lambda dst=dst, lo=lo: nc.scalar.copy(dst, PS[:, lo:lo + TT])),
                  reads=[R("PS", lo, lo + TT)], writes=[dreg])

        for c in range(KI):
            s2 = c % 2
            p.add("vector",
                  (lambda c=c, s2=s2: nc.vector.tensor_tensor(
                      tmp[:, s2 * 512:(s2 + 1) * 512], hbuf[:, c * TT:(c + 1) * TT],
                      PS[:, 0:512], ALU.mult)),
                  reads=[R("hbuf", c * TT, (c + 1) * TT), R("PS", 0, 512)],
                  writes=[R("tmp", s2 * 512, (s2 + 1) * 512)])
            p.add("vector",
                  (lambda c=c, s2=s2: nc.vector.tensor_tensor(
                      hpre[:, c * W + 3: c * W + 3 + TT],
                      tmp[:, s2 * 512:(s2 + 1) * 512], PS[:, 512:1024], ALU.add)),
                  reads=[R("tmp", s2 * 512, (s2 + 1) * 512), R("PS", 512, 1024)],
                  writes=[R("hpre", c * W + 3, c * W + 3 + TT)])

        if i == 0:
            p.add("vector", lambda: nc.vector.tensor_copy(
                lnbbf[:, :], cvec[:, 32:48]),
                reads=[R("cvec_sb", 32, 48)], writes=[R("lnbbf", 0, 16)])
            for j in range(4):
                sidx = SLOT[("fc", j)]
                bb = slot_base(sidx)
                p.add("gpsimd",
                      (lambda sidx=sidx, bb=bb: nc.gpsimd.dma_start(
                          ring[:, bb:bb + SLOTC], wimg_h[sidx])),
                      reads=[R("wimg", sidx * SLOTC, (sidx + 1) * SLOTC)],
                      writes=[R("ring", bb, bb + SLOTC)], dma="q_gp",
                      tag=f"cast{sidx}")
                for jj2 in range(2):
                    oc2 = 2 * j + jj2
                    lo = next_bank()
                    for c in range(KI):
                        la, lreg = slot_ap(("fc", j), c * 256 + jj2 * 128,
                                           c * 256 + (jj2 + 1) * 128)
                        p.add("tensor",
                              (lambda la=la, c=c, lo=lo:
                               nc.tensor.matmul(PS[0:128, lo:lo + 1], la,
                                                lnbbf[:, c:c + 1],
                                                start=(c == 0), stop=(c == KI - 1))),
                              reads=[lreg, R("lnbbf", c, c + 1)],
                              writes=[R("PS", lo, lo + TT)], tag=f"c1.{oc2}")
                    p.add("vector",
                          (lambda oc2=oc2, lo=lo: nc.vector.scalar_tensor_tensor(
                              fbias[:, oc2:oc2 + 1], PS[0:128, lo:lo + 1], 1.0,
                              cvec[:, 48 + oc2:49 + oc2], ALU.mult, ALU.add)),
                          reads=[R("PS", lo, lo + TT),
                                 R("cvec_sb", 48 + oc2, 49 + oc2)],
                          writes=[R("fbias", oc2, oc2 + 1)])
                for c in range(KI):
                    lo2 = c * 256
                    p.add("vector",
                          (lambda j=j, c=c, lo2=lo2: nc.vector.tensor_scalar_mul(
                              slot_ap(("fc", j), lo2, lo2 + 256)[0],
                              slot_ap(("fc", j), lo2, lo2 + 256)[0],
                              cvec[:, 16 + c:17 + c])),
                          reads=[R("ring", bb + lo2, bb + lo2 + 256),
                                 R("cvec_sb", 16 + c, 17 + c)],
                          writes=[R("ring", bb + lo2, bb + lo2 + 256)])
                p.add("sync",
                      (lambda sidx=sidx, bb=bb: nc.sync.dma_start(
                          wsc_h[sidx], ring[:, bb:bb + SLOTC])),
                      reads=[R("ring", bb, bb + SLOTC)],
                      writes=[R("wsc", sidx * SLOTC, (sidx + 1) * SLOTC)],
                      dma="q_sync", tag=f"wb{sidx}")
        hn_rhs = [(hpre[:, c * W + 3: c * W + 3 + TT],
                   R("hpre", c * W + 3, c * W + 3 + TT)) for c in range(KI)]
        for oc in range(8):
            if i != 0 and oc % 2 == 0:
                load_slot(i, ("fc", oc // 2))
            lo = next_bank()
            mm_job(wslot(("fc", oc // 2), KI, 256, oc % 2), hn_rhs, lo,
                   tag=f"fc{oc}")
            p.add("scalar",
                  (lambda oc=oc, lo=lo: nc.scalar.activation(
                      tfc[:, oc * TT:(oc + 1) * TT], PS[:, lo:lo + TT], AF.Silu,
                      bias=fbias[:, oc:oc + 1])),
                  reads=[R("PS", lo, lo + TT), R("fbias", oc, oc + 1)],
                  writes=[R("tfc", oc * TT, (oc + 1) * TT)])

        rot["banks"] = [1024, 1536, 2048, 2560, 512]

        tfc_rhs = [(tfc[:, c * TT:(c + 1) * TT], R("tfc", c * TT, (c + 1) * TT))
                   for c in range(KD)]
        for oc in range(KI):
            if oc % 4 == 0:
                load_slot(i, ("cp", oc // 4))
            lo = next_bank()
            mm_job(wslot(("cp", oc // 4), KD, 512, oc % 4), tfc_rhs, lo,
                   tag=f"cp{oc}")
            p.add("vector",
                  (lambda oc=oc, lo=lo: nc.vector.scalar_tensor_tensor(
                      gate[:, oc * TT:(oc + 1) * TT], PS[:, lo:lo + TT],
                      cvec[:, 56 + oc:57 + oc], hbuf[:, oc * TT:(oc + 1) * TT],
                      ALU.add, ALU.add)),
                  reads=[R("PS", lo, lo + TT), R("cvec_sb", 56 + oc, 57 + oc),
                         R("hbuf", oc * TT, (oc + 1) * TT)],
                  writes=[R("gate", oc * TT, (oc + 1) * TT)])

        rt3 = rtok[:, :].rearrange("p (s e) -> p s e", e=8)
        l23 = msk[:, 32:64].rearrange("p (s e) -> p s e", e=8)
        mk3 = msk[:, 0:32].rearrange("p (s e) -> p s e", e=8)
        rw3 = rwtok[:, :].rearrange("p (s e) -> p s e", e=8)
        m1b = rsc[:, 0:4].rearrange("p (s o) -> p s o", o=1).broadcast_to([128, 4, 8])
        m2b = rsc[:, 4:8].rearrange("p (s o) -> p s o", o=1).broadcast_to([128, 4, 8])
        p1b = rsc[:, 16:20].rearrange("p (s o) -> p s o", o=1).broadcast_to([128, 4, 8])
        p2b = rsc[:, 20:24].rearrange("p (s o) -> p s o", o=1).broadcast_to([128, 4, 8])
        p.add("vector", lambda: nc.vector.tensor_reduce(
            rsc[:, 0:4], rt3, AX.X, ALU.max),
            reads=[R("rtok", 0, 32)], writes=[R("rsc", 0, 4)])
        p.add("vector", lambda: nc.vector.tensor_tensor(
            mk3, rt3, m1b, ALU.is_equal),
            reads=[R("rtok", 0, 32), R("rsc", 0, 4)], writes=[R("msk", 0, 32)])
        p.add("vector", lambda: nc.vector.scalar_tensor_tensor(
            msk[:, 32:64], msk[:, 0:32], -1e30, rtok[:, :], ALU.mult, ALU.add),
            reads=[R("msk", 0, 32), R("rtok", 0, 32)], writes=[R("msk", 32, 64)])
        p.add("vector", lambda: nc.vector.tensor_reduce(
            rsc[:, 4:8], l23, AX.X, ALU.max),
            reads=[R("msk", 32, 64)], writes=[R("rsc", 4, 8)])
        p.add("vector", lambda: nc.vector.tensor_tensor(
            l23, l23, m2b, ALU.is_equal),
            reads=[R("msk", 32, 64), R("rsc", 4, 8)], writes=[R("msk", 32, 64)])
        p.add("vector", lambda: nc.vector.tensor_sub(
            rsc[:, 8:12], rsc[:, 4:8], rsc[:, 0:4]),
            reads=[R("rsc", 0, 8)], writes=[R("rsc", 8, 12)])
        p.add("scalar", lambda: nc.scalar.activation(
            rsc[:, 12:16], rsc[:, 8:12], AF.Exp),
            reads=[R("rsc", 8, 12)], writes=[R("rsc", 12, 16)])
        if i + 1 < n_tiles:
            ln1_stats(i + 1)
            ln1_apply_a(i + 1)
        p.add("vector", lambda: nc.vector.tensor_scalar_add(
            rsc[:, 16:20], rsc[:, 12:16], 1.0),
            reads=[R("rsc", 12, 16)], writes=[R("rsc", 16, 20)])
        p.add("vector", lambda: nc.vector.reciprocal(
            rsc[:, 16:20], rsc[:, 16:20]),
            reads=[R("rsc", 16, 20)], writes=[R("rsc", 16, 20)])
        p.add("vector", lambda: nc.vector.tensor_tensor(
            rsc[:, 20:24], rsc[:, 12:16], rsc[:, 16:20], ALU.mult),
            reads=[R("rsc", 12, 20)], writes=[R("rsc", 20, 24)])
        p.add("vector", lambda: nc.vector.tensor_tensor(
            rw3, mk3, p1b, ALU.mult),
            reads=[R("msk", 0, 32), R("rsc", 16, 20)], writes=[R("rwtok", 0, 32)])
        p.add("vector", lambda: nc.vector.tensor_tensor(
            mk3, l23, p2b, ALU.mult),
            reads=[R("msk", 32, 64), R("rsc", 20, 24)], writes=[R("msk", 0, 32)])
        p.add("vector", lambda: nc.vector.tensor_add(
            rwtok[:, :], rwtok[:, :], msk[:, 0:32]),
            reads=[R("rwtok", 0, 32), R("msk", 0, 32)], writes=[R("rwtok", 0, 32)])

        lo_t2 = next_bank()
        for s in range(4):
            p.add("tensor",
                  (lambda s=s, lo=lo_t2: nc.tensor.transpose(
                      PS[0:8, lo + s * 128:lo + (s + 1) * 128],
                      rwtok[:, s * 8:(s + 1) * 8], iden32[:, :])),
                  reads=[R("rwtok", s * 8, (s + 1) * 8), R("iden32", 0, 128)],
                  writes=[R("PS", lo_t2 + s * 128, lo_t2 + (s + 1) * 128)])
        p.add("scalar",
              (lambda lo=lo_t2: nc.scalar.copy(rwf[:, :], PS[0:8, lo:lo + TT])),
              reads=[R("PS", lo_t2, lo_t2 + TT)], writes=[R("rwf", 0, TT)])

        def emit_rw_tt(ex, ubase):
            heb = HE0 + (ex % 2) * 2048
            p.add("tensor",
                  (lambda ex=ex: nc.tensor.matmul(
                      PS[:, 0:512], eh[:, ex * 128:(ex + 1) * 128], rwf[0:8, :],
                      start=True, stop=True)),
                  reads=[R("eh", ex * 128, (ex + 1) * 128), R("rwf", 0, TT)],
                  writes=[R("PS", 0, 512)], tag=f"rw{ex}")
            for oc in range(4):
                ha, hreg = scr_reg(heb, oc)
                p.add("vector",
                      (lambda ha=ha, hreg=hreg: nc.vector.tensor_tensor(
                          ha, ha, PS[:, 0:512], ALU.mult)),
                      reads=[hreg, R("PS", 0, 512)], writes=[hreg])
            for oc in range(4):
                ha, hreg = scr_reg(heb, oc)
                ua, ureg = scr_reg(ubase, oc)
                p.add("vector",
                      (lambda ha=ha, ua=ua, hreg=hreg, ureg=ureg:
                       nc.vector.tensor_tensor(ha, ha, ua, ALU.mult)),
                      reads=[hreg, ureg], writes=[hreg])

        def emit_wd(ex):
            heb = HE0 + (ex % 2) * 2048
            he_rhs = [scr_reg(heb, c) for c in range(4)]
            for oc in range(8):
                if oc == 0:
                    load_slot(i, ("d", ex))
                lo = next_bank()
                mm_job(wslot(("d", ex), 4, 1024, oc), he_rhs, lo, tag=f"d{ex}.{oc}")
                p.add("vector",
                      (lambda oc=oc, lo=lo: nc.vector.tensor_tensor(
                          yacc[:, oc * TT:(oc + 1) * TT],
                          yacc[:, oc * TT:(oc + 1) * TT],
                          PS[:, lo:lo + TT], ALU.add)),
                      reads=[R("yacc", oc * TT, (oc + 1) * TT), R("PS", lo, lo + TT)],
                      writes=[R("yacc", oc * TT, (oc + 1) * TT)])

        emit_rw_tt(0, SG)

        h2_rhs = [(gate[:, c * TT:(c + 1) * TT], R("gate", c * TT, (c + 1) * TT))
                  for c in range(KI)]
        for oc in range(8):
            if oc % 2 == 0:
                load_slot(i, ("ow", oc // 2))
            lo = next_bank()
            mm_job(wslot(("ow", oc // 2), KI, 256, oc % 2), h2_rhs, lo,
                   tag=f"ow{oc}")
            p.add("scalar",
                  (lambda oc=oc, lo=lo: nc.scalar.copy(
                      yacc[:, oc * TT:(oc + 1) * TT], PS[:, lo:lo + TT])),
                  reads=[R("PS", lo, lo + TT)],
                  writes=[R("yacc", oc * TT, (oc + 1) * TT)])

        emit_rw_tt(1, UU)
        for ex in range(2, NE):
            heb = HE0 + (ex % 2) * 2048
            for oc in range(4):
                if oc == 0:
                    load_slot(i, ("g", ex))
                lo = next_bank()
                mm_job(wslot(("g", ex), KD, 512, oc), xnf_rhs(), lo,
                       tag=f"g{ex}.{oc}")
                dst, dreg = scr_reg(SG, oc)
                p.add("scalar",
                      (lambda dst=dst, lo=lo: nc.scalar.activation(
                          dst, PS[:, lo:lo + TT], AF.Silu)),
                      reads=[R("PS", lo, lo + TT)], writes=[dreg])
            for oc in range(4):
                if oc == 0:
                    load_slot(i, ("u", ex))
                lo = next_bank()
                mm_job(wslot(("u", ex), KD, 512, oc), xnf_rhs(), lo,
                       tag=f"u{ex}.{oc}")
                dst, dreg = scr_reg(UU, oc)
                p.add("scalar",
                      (lambda dst=dst, lo=lo: nc.scalar.copy(dst, PS[:, lo:lo + TT])),
                      reads=[R("PS", lo, lo + TT)], writes=[dreg])
            emit_wd(ex - 2)
            p.add("tensor",
                  (lambda ex=ex: nc.tensor.matmul(
                      PS[:, 0:512], eh[:, ex * 128:(ex + 1) * 128], rwf[0:8, :],
                      start=True, stop=True)),
                  reads=[R("eh", ex * 128, (ex + 1) * 128), R("rwf", 0, TT)],
                  writes=[R("PS", 0, 512)], tag=f"rw{ex}")
            for oc in range(4):
                sa, sreg = scr_reg(SG, oc)
                ha, hreg = scr_reg(heb, oc)
                p.add("vector",
                      (lambda sa=sa, ha=ha, sreg=sreg, hreg=hreg:
                       nc.vector.tensor_tensor(ha, sa, PS[:, 0:512], ALU.mult)),
                      reads=[sreg, R("PS", 0, 512)], writes=[hreg])
            for oc in range(4):
                ha, hreg = scr_reg(heb, oc)
                ua, ureg = scr_reg(UU, oc)
                p.add("vector",
                      (lambda ha=ha, ua=ua, hreg=hreg, ureg=ureg:
                       nc.vector.tensor_tensor(ha, ha, ua, ALU.mult)),
                      reads=[hreg, ureg], writes=[hreg])
        emit_wd(NE - 2)
        emit_wd(NE - 1)

        for oc in range(8):
            if oc % 4 == 0:
                load_slot(i, ("sg", oc // 4))
            lo = next_bank()
            mm_job(wslot(("sg", oc // 4), KD, 512, oc % 4), xnf_rhs(), lo,
                   tag=f"sg{oc}")
            p.add("scalar",
                  (lambda oc=oc, lo=lo: nc.scalar.activation(
                      scr[:, oc * TT:(oc + 1) * TT], PS[:, lo:lo + TT], AF.Silu)),
                  reads=[R("PS", lo, lo + TT)],
                  writes=[R("scr", oc * TT, (oc + 1) * TT)])
        for oc in range(8):
            if oc % 4 == 0:
                load_slot(i, ("su", oc // 4))
            lo = next_bank()
            mm_job(wslot(("su", oc // 4), KD, 512, oc % 4), xnf_rhs(), lo,
                   tag=f"su{oc}")
            p.add("vector",
                  (lambda oc=oc, lo=lo: nc.vector.tensor_tensor(
                      scr[:, 4096 + oc * TT:4096 + (oc + 1) * TT],
                      scr[:, oc * TT:(oc + 1) * TT], PS[:, lo:lo + TT], ALU.mult)),
                  reads=[R("scr", oc * TT, (oc + 1) * TT), R("PS", lo, lo + TT)],
                  writes=[R("scr", 4096 + oc * TT, 4096 + (oc + 1) * TT)])

        if i + 1 < n_tiles:
            ln1_apply(i + 1)

        prod_rhs = [(scr[:, 4096 + c * TT:4096 + (c + 1) * TT],
                     R("scr", 4096 + c * TT, 4096 + (c + 1) * TT))
                    for c in range(KD)]
        for oc in range(8):
            if oc % 4 == 0:
                load_slot(i, ("sd", oc // 4))
            lo = next_bank()
            mm_job(wslot(("sd", oc // 4), KD, 512, oc % 4), prod_rhs, lo,
                   tag=f"sd{oc}")
            p.add("vector",
                  (lambda oc=oc, lo=lo: nc.vector.tensor_tensor(
                      yacc[:, oc * TT:(oc + 1) * TT],
                      yacc[:, oc * TT:(oc + 1) * TT],
                      PS[:, lo:lo + TT], ALU.add)),
                  reads=[R("yacc", oc * TT, (oc + 1) * TT), R("PS", lo, lo + TT)],
                  writes=[R("yacc", oc * TT, (oc + 1) * TT)])
            p.add("scalar",
                  (lambda oc=oc: nc.scalar.copy(
                      scr[:, oc * TT:(oc + 1) * TT], yacc[:, oc * TT:(oc + 1) * TT])),
                  reads=[R("yacc", oc * TT, (oc + 1) * TT)],
                  writes=[R("scr", oc * TT, (oc + 1) * TT)])

        def make_yfin(i=i):
            par = i % 2
            for s in range(4):
                half = s % 2
                for c in range(8):
                    p.add("tensor",
                          (lambda s=s, c=c, half=half: nc.tensor.transpose(
                              PT[:, half * 1024 + c * 128: half * 1024 + (c + 1) * 128],
                              scr[:, c * TT + s * 128: c * TT + (s + 1) * 128],
                              iden[:, :])),
                          reads=[R("scr", c * TT + s * 128, c * TT + (s + 1) * 128),
                                 R("iden", 0, 128)],
                          writes=[R("PT", half * 1024 + c * 128,
                                    half * 1024 + (c + 1) * 128)])
                p.add("vector",
                      (lambda i=i, s=s, half=half: nc.vector.tensor_tensor(
                          xsub(i, s), xsub(i, s),
                          PT[:, half * 1024:(half + 1) * 1024], ALU.add)),
                      reads=[xreg(i, s), R("PT", half * 1024, (half + 1) * 1024)],
                      writes=[xreg(i, s)])
            dst = y_h[i * TT:(i + 1) * TT, :].rearrange("(s p) d -> p s d", p=128)
            ysrc = xtok[:, par * 4096:(par + 1) * 4096].rearrange(
                "p (s d) -> p s d", d=D)
            p.add("gpsimd",
                  (lambda dst=dst, ysrc=ysrc: nc.gpsimd.dma_start(dst, ysrc)),
                  reads=[R("xtok", par * 4096, (par + 1) * 4096)],
                  writes=[R("y_h", i * TT * D, (i + 1) * TT * D)],
                  dma="q_gp", tag=f"y{i}")
        pend_yfin[0] = make_yfin

    if pend_yfin[0] is not None:
        pend_yfin[0]()
        pend_yfin[0] = None
    counts = dict(p.dma_counts)
    p.add("gpsimd", lambda: nc.gpsimd.nop(), reads=[R("y_h", 0, T * D)],
          tag="final")
    p.finalize()

    per_engine = {en: [op for op in p.ops if op["eng"] == en] for en in ENGINES}

    with ctx:
        with nc.Block() as block:
            def emit_stream(en):
                eng_obj = getattr(nc, en)
                for op in per_engine[en]:
                    for sem_name, val in op["waits"]:
                        eng_obj.wait_ge(sems[sem_name], val)
                    inst = op["emit"]()
                    if op["dma"] is not None:
                        inst.then_inc(sems[op["dma"]], 16)
                    elif op["inc"]:
                        inst.then_inc(sems["e_" + en], 1)

            @block.tensor
            def _(t):
                emit_stream("tensor")

            @block.vector
            def _(v):
                emit_stream("vector")

            @block.scalar
            def _(s):
                emit_stream("scalar")

            @block.sync
            def _(sy):
                emit_stream("sync")
                for ln in range(NLANE):
                    sy.wait_ge(sems[f"q_sync{ln}"], counts.get(f"q_sync{ln}", 0))

            @block.gpsimd
            def _(g):
                emit_stream("gpsimd")
                for ln in range(NLANE):
                    g.wait_ge(sems[f"q_gp{ln}"], counts.get(f"q_gp{ln}", 0))
    return nc



def _img(Wm, kchunks, sl=None):
    d_in, d_out = Wm.shape
    assert d_in == kchunks * 128
    V = Wm.reshape(kchunks, 128, d_out)
    if sl is not None:
        V = V[:, :, sl]
    return np.ascontiguousarray(V.transpose(1, 0, 2).reshape(128, -1))


def _prep(weights):
    (ipw, convw, convb, lng, lnb, fcw, fcb, cpw, cpb, oww,
     wr, wg, wu, wd, sg, su, sd, eb) = weights
    imgs = {}
    ipT = np.ascontiguousarray(ipw.T)
    for j in range(8):
        imgs[("ip", j)] = _img(ipT, 8, slice(j * 512, (j + 1) * 512))
    for j in range(4):
        img = np.empty((128, 4096), np.float32)
        for cc in range(2):
            for k in range(4):
                for og in range(2):
                    for oc2 in range(2):
                        col = cc * 2048 + k * 512 + og * 256 + oc2 * 128
                        blk = convw[(2 * j + og) * 256 + oc2 * 128:
                                    (2 * j + og) * 256 + (oc2 + 1) * 128,
                                    cc * 128:(cc + 1) * 128, k]
                        img[:, col:col + 128] = blk.T
        imgs[("cv", j)] = img
    fcT = np.ascontiguousarray(fcw.T)
    for j in range(4):
        imgs[("fc", j)] = _img(fcT, 16, slice(j * 256, (j + 1) * 256))
    cpT = np.ascontiguousarray(cpw.T)
    for j in range(4):
        imgs[("cp", j)] = _img(cpT, 8, slice(j * 512, (j + 1) * 512))
    owT = np.ascontiguousarray(oww.T)
    for j in range(4):
        imgs[("ow", j)] = _img(owT, 16, slice(j * 256, (j + 1) * 256))
    for ex in range(8):
        imgs[("g", ex)] = _img(wg[ex], 8)
        imgs[("u", ex)] = _img(wu[ex], 8)
        imgs[("d", ex)] = _img(wd[ex], 4)
    for j in range(2):
        imgs[("sg", j)] = _img(sg, 8, slice(j * 512, (j + 1) * 512))
        imgs[("su", j)] = _img(su, 8, slice(j * 512, (j + 1) * 512))
        imgs[("sd", j)] = _img(sd, 8, slice(j * 512, (j + 1) * 512))
    wimg = np.stack([imgs[k] for k in SLOT_ORDER], axis=0)
    assert wimg.shape == (NSLOT, 128, SLOTC), wimg.shape

    cvecs = np.zeros((128, 80), np.float32)
    cvecs[:, 0:16] = convb.reshape(16, 128).T
    cvecs[:, 16:32] = lng.reshape(16, 128).T
    cvecs[:, 32:48] = lnb.reshape(16, 128).T
    cvecs[:, 48:56] = fcb.reshape(8, 128).T
    cvecs[:, 56:72] = cpb.reshape(16, 128).T
    cvecs[0:8, 72] = eb
    wrimg = np.zeros((128, 64), np.float32)
    for c in range(8):
        wrimg[:, c * 8:(c + 1) * 8] = wr[c * 128:(c + 1) * 128, :]
    return wimg, cvecs, wrimg


def kernel(L0_attn_hidden_states, L0_attn_in_proj_w, L0_attn_conv_w,
           L0_attn_conv_b, L0_attn_ln_g, L0_attn_ln_b, L0_attn_fc_w,
           L0_attn_fc_b, L0_attn_cproj_w, L0_attn_cproj_b, L0_attn_out_w,
           L0_ffn_Wr, L0_ffn_Wg, L0_ffn_Wu, L0_ffn_Wd, L0_ffn_Sg, L0_ffn_Su,
           L0_ffn_Sd, L0_ffn_expert_bias):
    x = np.ascontiguousarray(np.asarray(L0_attn_hidden_states, dtype=np.float32))
    B, T, _ = x.shape
    weights = [np.ascontiguousarray(np.asarray(w_, dtype=np.float32)) for w_ in (
        L0_attn_in_proj_w, L0_attn_conv_w, L0_attn_conv_b, L0_attn_ln_g,
        L0_attn_ln_b, L0_attn_fc_w, L0_attn_fc_b, L0_attn_cproj_w,
        L0_attn_cproj_b, L0_attn_out_w, L0_ffn_Wr, L0_ffn_Wg, L0_ffn_Wu,
        L0_ffn_Wd, L0_ffn_Sg, L0_ffn_Su, L0_ffn_Sd, L0_ffn_expert_bias)]
    wimg, cvecs, wrimg = _prep(weights)
    nc = build_nc(T // TT)
    in_maps = [{"x": x[c], "wimg": wimg, "cvec": cvecs, "wrimg": wrimg}
               for c in range(B)]
    res = run_bass_kernel_spmd(nc, in_maps, core_ids=list(range(B)))
    return np.stack([r["y"] for r in res.results], axis=0)


# version 13
# speedup vs baseline: 1.1810x; 1.0387x over previous
import numpy as np
from contextlib import ExitStack
import concourse.bass as bass
import concourse.mybir as mybir
from concourse.bass_utils import run_bass_kernel_spmd

F32 = mybir.dt.float32
BF16 = mybir.dt.bfloat16
AF = mybir.ActivationFunctionType
ALU = mybir.AluOpType
AX = mybir.AxisListType

D = 1024
I = 2048
TT = 512
KD = D // 128
KI = I // 128
NE = 8
NSLOT = 54
SLOTC = 4096
RING = 6
W = TT + 3
NLANE = 4

ENGINES = ("tensor", "vector", "scalar", "gpsimd", "sync")


def slot_order():
    order = [("ip", j) for j in range(8)]
    order += [("cv", j) for j in range(4)]
    order += [("g", 0), ("u", 0), ("g", 1), ("u", 1)]
    order += [("fc", j) for j in range(4)]
    order += [("cp", j) for j in range(4)]
    order += [("ow", j) for j in range(4)]
    for ex in range(2, NE):
        order += [("g", ex), ("u", ex), ("d", ex - 2)]
    order += [("d", 6), ("d", 7)]
    order += [("sg", 0), ("sg", 1), ("su", 0), ("su", 1), ("sd", 0), ("sd", 1)]
    return order


SLOT_ORDER = slot_order()
SLOT = {k: idx for idx, k in enumerate(SLOT_ORDER)}
assert len(SLOT_ORDER) == NSLOT


class Plan:

    def __init__(self):
        self.ops = []
        self.writes = {}
        self.reads = {}
        self.dma_counts = {}
        self.dma_rr = {}

    @staticmethod
    def _overlap(recs, lo, hi):
        return [r for r in recs if r[0] < hi and lo < r[1]]

    def add(self, eng, emit, reads=(), writes=(), dma=None, tag=""):
        opid = len(self.ops)
        deps = set()
        for key, lo, hi in reads:
            for _, _, w in self._overlap(self.writes.get(key, ()), lo, hi):
                deps.add(w)
        for key, lo, hi in writes:
            for _, _, w in self._overlap(self.writes.get(key, ()), lo, hi):
                deps.add(w)
            for _, _, r in self._overlap(self.reads.get(key, ()), lo, hi):
                deps.add(r)
        dma_val = None
        dma_gate = None
        if dma is not None:
            lane = self.dma_rr.get(dma, 0)
            self.dma_rr[dma] = (lane + 1) % NLANE
            dma = f"{dma}{lane}"
            prev = self.dma_counts.get(dma, 0)
            if prev > 0:
                dma_gate = (dma, prev)
            self.dma_counts[dma] = prev + 16
            dma_val = self.dma_counts[dma]
        self.ops.append(dict(id=opid, eng=eng, emit=emit, deps=deps, dma=dma,
                             dma_val=dma_val, dma_gate=dma_gate, tag=tag))
        for key, lo, hi in writes:
            wr = self.writes.setdefault(key, [])
            wr[:] = [r for r in wr if not (lo <= r[0] and r[1] <= hi)]
            wr.append((lo, hi, opid))
            rd = self.reads.setdefault(key, [])
            rd[:] = [r for r in rd if not (lo <= r[0] and r[1] <= hi)]
        for key, lo, hi in reads:
            self.reads.setdefault(key, []).append((lo, hi, opid))
        return opid

    def finalize(self):
        ops = self.ops
        for op in ops:
            best = {}
            for d in op["deps"]:
                dop = ops[d]
                if (dop["eng"] == "tensor" and op["eng"] == "tensor"
                        and dop["dma"] is None and op["dma"] is None):
                    continue
                key = (dop["eng"], dop["dma"])
                if key not in best or d > best[key]:
                    best[key] = d
            op["deps"] = set(best.values())
        targets = set()
        for op in ops:
            for d in op["deps"]:
                if ops[d]["dma"] is None:
                    targets.add(d)
        rank = {}
        counts = {e: 0 for e in ENGINES}
        for op in ops:
            if op["dma"] is None and op["id"] in targets:
                counts[op["eng"]] += 1
                rank[op["id"]] = counts[op["eng"]]
                op["inc"] = True
            else:
                op["inc"] = op["dma"] is not None
        highwater = {e: {} for e in ENGINES}
        for op in ops:
            waits = []
            cands = []
            for d in sorted(op["deps"]):
                dop = ops[d]
                if dop["dma"] is not None:
                    cands.append((dop["dma"], dop["dma_val"]))
                else:
                    cands.append(("e_" + dop["eng"], rank[d]))
            if op["dma_gate"] is not None:
                cands.append(op["dma_gate"])
            for sem, val in cands:
                hw = highwater[op["eng"]]
                if hw.get(sem, 0) < val:
                    hw[sem] = val
                    waits.append((sem, val))
            op["waits"] = waits
        return counts


def build_nc(n_tiles):
    T = n_tiles * TT
    p = Plan()
    nc = bass.Bass("TRN2", target_bir_lowering=False)

    x_h = nc.dram_tensor("x", [T, D], F32, kind="ExternalInput").ap()
    wimg_h = nc.dram_tensor("wimg", [NSLOT, 128, SLOTC], F32, kind="ExternalInput").ap()
    cvec_h = nc.dram_tensor("cvec", [128, 80], F32, kind="ExternalInput").ap()
    wr_h = nc.dram_tensor("wrimg", [128, 64], F32, kind="ExternalInput").ap()
    y_h = nc.dram_tensor("y", [T, D], F32, kind="ExternalOutput").ap()
    wsc_h = nc.dram_tensor("wsc", [NSLOT, 128, SLOTC], BF16, kind="Internal").ap()

    ctx = ExitStack()
    e = ctx.enter_context
    xtok = e(nc.sbuf_tensor("xtok", [128, 2 * 4096], F32))
    xntok = e(nc.sbuf_tensor("xntok", [128, 4096], BF16))
    xnf = e(nc.sbuf_tensor("xnf", [128, 4096], BF16))
    hpre = e(nc.sbuf_tensor("hpre", [128, KI * W], BF16))
    gate = e(nc.sbuf_tensor("gate", [128, 8192], BF16))
    hbuf = e(nc.sbuf_tensor("hbuf", [128, 8192], BF16))
    tfc = e(nc.sbuf_tensor("tfc", [128, 4096], BF16))
    scr = e(nc.sbuf_tensor("scr", [128, 8192], BF16))
    yacc = e(nc.sbuf_tensor("yacc", [128, 4096], F32))
    sqscr = e(nc.sbuf_tensor("sqscr", [128, 2048], BF16))
    tmp = e(nc.sbuf_tensor("tmp", [128, 1024], F32))
    ring = e(nc.sbuf_tensor("ring", [128, RING * SLOTC], BF16))
    cvec = e(nc.sbuf_tensor("cvec_sb", [128, 80], F32))
    wrsb = e(nc.sbuf_tensor("wrsb", [128, 64], BF16))
    ones = e(nc.sbuf_tensor("ones_sb", [128, 2], BF16))
    onesrow = e(nc.sbuf_tensor("onesrow", [1, 128], BF16))
    iden = e(nc.sbuf_tensor("iden", [128, 128], BF16))
    iden32 = e(nc.sbuf_tensor("iden32", [128, 128], F32))
    eh = e(nc.sbuf_tensor("eh", [8, 1024], BF16))
    rows = e(nc.sbuf_tensor("rows", [1, 1536], F32))
    abbf = e(nc.sbuf_tensor("abbf", [1, 1024], BF16))
    lgsb = e(nc.sbuf_tensor("lgsb", [8, TT], F32))
    rtok = e(nc.sbuf_tensor("rtok", [128, 32], F32))
    msk = e(nc.sbuf_tensor("msk", [128, 64], F32))
    rwtok = e(nc.sbuf_tensor("rwtok", [128, 32], F32))
    rwf = e(nc.sbuf_tensor("rwf", [8, TT], BF16))
    rsc = e(nc.sbuf_tensor("rsc", [128, 24], F32))
    mln1 = e(nc.sbuf_tensor("mln1", [128, 4], F32))
    rln1 = e(nc.sbuf_tensor("rln1", [128, 4], F32))
    sln1 = e(nc.sbuf_tensor("sln1", [128, 4], F32))
    qln1 = e(nc.sbuf_tensor("qln1", [128, 4], F32))
    vsc = e(nc.sbuf_tensor("vsc", [128, 8], F32))
    consts = e(nc.sbuf_tensor("consts", [128, 4], F32))
    fbias = e(nc.sbuf_tensor("fbias", [128, 8], F32))
    lnbbf = e(nc.sbuf_tensor("lnbbf", [128, 16], BF16))
    PS = e(nc.psum_tensor("PS", [128, 3072], F32))
    PT = e(nc.psum_tensor("PT", [128, 2048], BF16))

    nc.const_aps.register(F32, 1e-5, consts[:, 0:1])
    nc.const_aps.register(F32, -0.5, consts[:, 1:2])
    nc.const_aps.register(F32, -1.0, consts[:, 2:3])

    sems = {}
    sem_names = ["e_tensor", "e_vector", "e_scalar", "e_gpsimd", "e_sync"]
    sem_names += [f"q_sync{i}" for i in range(NLANE)]
    sem_names += [f"q_gp{i}" for i in range(NLANE)]
    for name in sem_names:
        sems[name] = e(nc.semaphore(name))

    def R(t, lo, hi):
        return (t, lo, hi)

    rot = {"banks": [1024, 1536, 2048, 2560], "i": 0}

    def next_bank():
        b = rot["banks"][rot["i"] % len(rot["banks"])]
        rot["i"] += 1
        return b

    def mm_job(lhsT, rhs, out_lo, m_hi=128, tag=""):
        n = len(lhsT)
        for k in range(n):
            st, sp = (k == 0), (k == n - 1)
            la, lreg = lhsT[k]
            ra, rreg = rhs[k]
            p.add("tensor",
                  (lambda la=la, ra=ra, st=st, sp=sp, lo=out_lo, mh=m_hi:
                   nc.tensor.matmul(PS[0:mh, lo:lo + TT], la, ra, start=st, stop=sp)),
                  reads=[lreg, rreg], writes=[R("PS", out_lo, out_lo + TT)], tag=tag)

    def slot_base(s):
        return (s % RING) * SLOTC

    def load_slot(tile, key):
        s = SLOT[key]
        b = slot_base(s)
        if tile == 0:
            p.add("gpsimd",
                  (lambda s=s, b=b: nc.gpsimd.dma_start(
                      ring[:, b:b + SLOTC], wimg_h[s])),
                  reads=[R("wimg", s * SLOTC, (s + 1) * SLOTC)],
                  writes=[R("ring", b, b + SLOTC)], dma="q_gp", tag=f"cast{s}")
            p.add("sync",
                  (lambda s=s, b=b: nc.sync.dma_start(
                      wsc_h[s], ring[:, b:b + SLOTC])),
                  reads=[R("ring", b, b + SLOTC)],
                  writes=[R("wsc", s * SLOTC, (s + 1) * SLOTC)],
                  dma="q_sync", tag=f"wb{s}")
        else:
            p.add("sync",
                  (lambda s=s, b=b: nc.sync.dma_start(
                      ring[:, b:b + SLOTC], wsc_h[s])),
                  reads=[R("wsc", s * SLOTC, (s + 1) * SLOTC)],
                  writes=[R("ring", b, b + SLOTC)], dma="q_sync", tag=f"ld{s}")

    def slot_ap(key, lo, hi):
        b = slot_base(SLOT[key])
        return (ring[:, b + lo:b + hi], R("ring", b + lo, b + hi))

    def wslot(key, kchunks, ow, jj):
        return [slot_ap(key, c * ow + jj * 128, c * ow + (jj + 1) * 128)
                for c in range(kchunks)]

    def xnf_rhs():
        return [(xnf[:, c * TT:(c + 1) * TT], R("xnf", c * TT, (c + 1) * TT))
                for c in range(KD)]

    SG, UU, HE0 = 0, 2048, 4096

    def scr_reg(base, oc):
        return (scr[:, base + oc * TT:base + (oc + 1) * TT],
                R("scr", base + oc * TT, base + (oc + 1) * TT))

    p.add("sync", lambda: nc.sync.dma_start(cvec[:, :], cvec_h),
          writes=[R("cvec_sb", 0, 80)], dma="q_sync")
    p.add("gpsimd", lambda: nc.gpsimd.dma_start(wrsb[:, :], wr_h),
          writes=[R("wrsb", 0, 64)], dma="q_gp")
    p.add("gpsimd", lambda: nc.gpsimd.memset(ones[:, :], 1.0),
          writes=[R("ones_sb", 0, 2)])
    p.add("gpsimd", lambda: nc.gpsimd.memset(onesrow[:, :], 1.0),
          writes=[R("onesrow", 0, 128)])
    p.add("gpsimd", lambda: nc.gpsimd.memset(consts[:, 0:1], 1e-5),
          writes=[R("consts", 0, 1)])
    p.add("gpsimd", lambda: nc.gpsimd.memset(consts[:, 1:2], -0.5),
          writes=[R("consts", 1, 2)])
    p.add("gpsimd", lambda: nc.gpsimd.memset(consts[:, 2:3], -1.0),
          writes=[R("consts", 2, 3)])
    p.add("gpsimd", lambda: nc.gpsimd.iota(tmp[:, 0:128], [[0, 128]],
          channel_multiplier=1, allow_small_or_imprecise_dtypes=True),
          writes=[R("tmp", 0, 128)])
    p.add("gpsimd", lambda: nc.gpsimd.iota(tmp[:, 512:640], [[1, 128]],
          channel_multiplier=0, allow_small_or_imprecise_dtypes=True),
          writes=[R("tmp", 512, 640)])
    p.add("vector", lambda: nc.vector.tensor_tensor(
        iden[:, :], tmp[:, 0:128], tmp[:, 512:640], ALU.is_equal),
        reads=[R("tmp", 0, 128), R("tmp", 512, 640)], writes=[R("iden", 0, 128)])
    p.add("vector", lambda: nc.vector.tensor_tensor(
        iden32[:, :], tmp[:, 0:128], tmp[:, 512:640], ALU.is_equal),
        reads=[R("tmp", 0, 128), R("tmp", 512, 640)], writes=[R("iden32", 0, 128)])
    p.add("gpsimd", lambda: nc.gpsimd.iota(tmp[0:8, 0:1024], [[1, 8], [0, 128]],
          channel_multiplier=0, allow_small_or_imprecise_dtypes=True),
          writes=[R("tmp", 0, 1024)])
    p.add("gpsimd", lambda: nc.gpsimd.iota(yacc[0:8, 0:1024], [[0, 8], [0, 128]],
          channel_multiplier=1, allow_small_or_imprecise_dtypes=True),
          writes=[R("yacc", 0, 1024)])
    p.add("vector", lambda: nc.vector.tensor_tensor(
        eh[:, :], tmp[0:8, 0:1024], yacc[0:8, 0:1024], ALU.is_equal),
        reads=[R("tmp", 0, 1024), R("yacc", 0, 1024)], writes=[R("eh", 0, 1024)])
    p.add("gpsimd", lambda: nc.gpsimd.memset(
        hpre[:, :].rearrange("p (c w) -> p c w", w=W)[:, :, 0:3], 0.0),
        writes=[R("hpre", 0, KI * W)])

    def load_x(i):
        par = i % 2
        src = x_h[i * TT:(i + 1) * TT, :].rearrange("(s p) d -> p s d", p=128)
        dst = xtok[:, par * 4096:(par + 1) * 4096].rearrange(
            "p (s d) -> p s d", d=D)
        p.add("sync",
              (lambda src=src, dst=dst: nc.sync.dma_start(dst, src)),
              writes=[R("xtok", par * 4096, (par + 1) * 4096)],
              dma="q_sync", tag=f"x{i}")

    def xreg(i, s):
        par = i % 2
        return R("xtok", par * 4096 + s * 1024, par * 4096 + (s + 1) * 1024)

    def xsub(i, s):
        par = i % 2
        return xtok[:, par * 4096 + s * 1024: par * 4096 + (s + 1) * 1024]

    def ln1_stats(i):
        for s in range(4):
            p.add("vector",
                  (lambda i=i, s=s: nc.vector.reduce_sum(
                      sln1[:, s:s + 1], xsub(i, s), AX.X)),
                  reads=[xreg(i, s)], writes=[R("sln1", s, s + 1)])
            p.add("scalar",
                  (lambda i=i, s=s: nc.scalar.activation(
                      sqscr[:, 0:1024], xsub(i, s), AF.Square,
                      accum_out=qln1[:, s:s + 1])),
                  reads=[xreg(i, s)],
                  writes=[R("sqscr", 0, 1024), R("qln1", s, s + 1)])
        p.add("vector", lambda: nc.vector.tensor_scalar_mul(
            mln1[:, 0:4], sln1[:, 0:4], 1.0 / D),
            reads=[R("sln1", 0, 4)], writes=[R("mln1", 0, 4)])
        p.add("vector", lambda: nc.vector.tensor_tensor(
            vsc[:, 0:4], mln1[:, 0:4], mln1[:, 0:4], ALU.mult),
            reads=[R("mln1", 0, 4)], writes=[R("vsc", 0, 4)])
        p.add("vector", lambda: nc.vector.scalar_tensor_tensor(
            vsc[:, 4:8], qln1[:, 0:4], 1.0 / D, vsc[:, 0:4],
            ALU.mult, ALU.subtract),
            reads=[R("qln1", 0, 4), R("vsc", 0, 4)], writes=[R("vsc", 4, 8)])
        p.add("scalar", lambda: nc.scalar.activation(
            rln1[:, 0:4], vsc[:, 4:8], AF.Ln, bias=1e-5),
            reads=[R("vsc", 4, 8), R("consts", 0, 1)], writes=[R("rln1", 0, 4)])
        p.add("scalar", lambda: nc.scalar.activation(
            rln1[:, 0:4], rln1[:, 0:4], AF.Exp, scale=-0.5),
            reads=[R("rln1", 0, 4), R("consts", 1, 2)], writes=[R("rln1", 0, 4)])

    def ln1_apply_a(i):
        for s in range(4):
            p.add("vector",
                  (lambda i=i, s=s: nc.vector.tensor_scalar(
                      xntok[:, s * 1024:(s + 1) * 1024], xsub(i, s),
                      mln1[:, s:s + 1], rln1[:, s:s + 1],
                      ALU.subtract, ALU.mult)),
                  reads=[xreg(i, s), R("mln1", s, s + 1), R("rln1", s, s + 1)],
                  writes=[R("xntok", s * 1024, (s + 1) * 1024)])

    def ln1_apply(i, subtiles=range(4)):
        for s in subtiles:
            half = s % 2
            for c in range(8):
                p.add("tensor",
                      (lambda s=s, c=c, half=half: nc.tensor.transpose(
                          PT[:, half * 1024 + c * 128: half * 1024 + (c + 1) * 128],
                          xntok[:, s * 1024 + c * 128: s * 1024 + (c + 1) * 128],
                          iden[:, :])),
                      reads=[R("xntok", s * 1024 + c * 128, s * 1024 + (c + 1) * 128),
                             R("iden", 0, 128)],
                      writes=[R("PT", half * 1024 + c * 128,
                                half * 1024 + (c + 1) * 128)])
            out_ap = xnf[:, :].rearrange("p (c sb t) -> p c sb t",
                                         c=8, sb=4, t=128)[:, :, s, :]
            in_ap = PT[:, half * 1024:(half + 1) * 1024].rearrange(
                "p (c t) -> p c t", t=128)
            p.add("scalar",
                  (lambda out_ap=out_ap, in_ap=in_ap:
                   nc.scalar.copy(out_ap, in_ap)),
                  reads=[R("PT", half * 1024, (half + 1) * 1024)],
                  writes=[R("xnf", 0, 4096)])

    load_x(0)
    ln1_stats(0)
    ln1_apply_a(0)
    ln1_apply(0)

    pend_yfin = [None]
    for i in range(n_tiles):
        rot["banks"] = [1024, 1536, 2048, 2560]

        for oc in range(32):
            if oc % 4 == 0:
                load_slot(i, ("ip", oc // 4))
            lo = next_bank()
            mm_job(wslot(("ip", oc // 4), KD, 512, oc % 4), xnf_rhs(), lo,
                   tag=f"ip{oc}")
            if oc < 16:
                p.add("scalar",
                      (lambda oc=oc, lo=lo: nc.scalar.copy(
                          hpre[:, oc * W + 3: oc * W + 3 + TT], PS[:, lo:lo + TT])),
                      reads=[R("PS", lo, lo + TT)],
                      writes=[R("hpre", oc * W + 3, oc * W + 3 + TT)])
            else:
                g = oc - 16
                p.add("scalar",
                      (lambda g=g, lo=lo: nc.scalar.activation(
                          gate[:, g * TT:(g + 1) * TT], PS[:, lo:lo + TT], AF.Silu)),
                      reads=[R("PS", lo, lo + TT)],
                      writes=[R("gate", g * TT, (g + 1) * TT)])

        if pend_yfin[0] is not None:
            pend_yfin[0]()
            pend_yfin[0] = None

        def stat_chunk(c):
            sq = c % 4
            p.add("scalar",
                  (lambda c=c, sq=sq: nc.scalar.activation(
                      sqscr[:, sq * TT:(sq + 1) * TT],
                      hbuf[:, c * TT:(c + 1) * TT], AF.Square)),
                  reads=[R("hbuf", c * TT, (c + 1) * TT)],
                  writes=[R("sqscr", sq * TT, (sq + 1) * TT)])
            p.add("tensor",
                  (lambda c=c: nc.tensor.matmul(
                      PS[0:1, 0:512], ones[:, 0:1], hbuf[:, c * TT:(c + 1) * TT],
                      start=(c == 0), stop=(c == KI - 1))),
                  reads=[R("ones_sb", 0, 1), R("hbuf", c * TT, (c + 1) * TT)],
                  writes=[R("PS", 0, 512)], tag="S1")
            p.add("tensor",
                  (lambda c=c, sq=sq: nc.tensor.matmul(
                      PS[0:1, 512:1024], ones[:, 0:1],
                      sqscr[:, sq * TT:(sq + 1) * TT],
                      start=(c == 0), stop=(c == KI - 1))),
                  reads=[R("ones_sb", 0, 1), R("sqscr", sq * TT, (sq + 1) * TT)],
                  writes=[R("PS", 512, 1024)], tag="S2")

        for oc in range(KI):
            if oc % 4 == 0:
                load_slot(i, ("cv", oc // 4))
            g = oc // 2
            oc2 = oc % 2
            og = g % 2
            lh, rh = [], []
            for cc in range(2):
                gc = 2 * g + cc
                for k in range(4):
                    lh.append(slot_ap(("cv", g // 2),
                                      cc * 2048 + k * 512 + og * 256 + oc2 * 128,
                                      cc * 2048 + k * 512 + og * 256 + (oc2 + 1) * 128))
                    rh.append((hpre[:, gc * W + k: gc * W + k + TT],
                               R("hpre", gc * W + k, gc * W + k + TT)))
            lo = next_bank()
            mm_job(lh, rh, lo, tag=f"cv{oc}")
            p.add("vector",
                  (lambda oc=oc, lo=lo: nc.vector.scalar_tensor_tensor(
                      hbuf[:, oc * TT:(oc + 1) * TT], PS[:, lo:lo + TT],
                      cvec[:, oc:oc + 1], gate[:, oc * TT:(oc + 1) * TT],
                      ALU.add, ALU.mult)),
                  reads=[R("PS", lo, lo + TT), R("cvec_sb", oc, oc + 1),
                         R("gate", oc * TT, (oc + 1) * TT)],
                  writes=[R("hbuf", oc * TT, (oc + 1) * TT)])
            if oc >= 3:
                stat_chunk(oc - 3)
        for c_ in range(KI - 3, KI):
            stat_chunk(c_)

        if i + 1 < n_tiles:
            load_x(i + 1)
            src = hpre[:, :].rearrange("p (c w) -> p c w", w=W)[:, :, TT:TT + 3]
            dst = hpre[:, :].rearrange("p (c w) -> p c w", w=W)[:, :, 0:3]
            p.add("vector",
                  (lambda src=src, dst=dst: nc.vector.tensor_copy(dst, src)),
                  reads=[R("hpre", 0, KI * W)], writes=[R("hpre", 0, KI * W)],
                  tag="halo")

        lo_lg = next_bank()
        mm_job([(wrsb[:, c * 8:(c + 1) * 8], R("wrsb", c * 8, (c + 1) * 8))
                for c in range(KD)], xnf_rhs(), lo_lg, m_hi=8, tag="lg")
        p.add("vector",
              (lambda lo=lo_lg: nc.vector.tensor_scalar_add(
                  lgsb[:, :], PS[0:8, lo:lo + TT], cvec[0:8, 72:73])),
              reads=[R("PS", lo_lg, lo_lg + TT), R("cvec_sb", 72, 73)],
              writes=[R("lgsb", 0, TT)])
        lo_t = next_bank()
        for s in range(4):
            p.add("tensor",
                  (lambda s=s, lo=lo_t: nc.tensor.transpose(
                      PS[:, lo + s * 8:lo + (s + 1) * 8],
                      lgsb[0:8, s * 128:(s + 1) * 128], iden32[0:8, 0:8])),
                  reads=[R("lgsb", s * 128, (s + 1) * 128), R("iden32", 0, 8)],
                  writes=[R("PS", lo_t + s * 8, lo_t + (s + 1) * 8)])
        p.add("vector",
              (lambda lo=lo_t: nc.vector.tensor_copy(rtok[:, :], PS[:, lo:lo + 32])),
              reads=[R("PS", lo_t, lo_t + 32)], writes=[R("rtok", 0, 32)])

        p.add("vector", lambda: nc.vector.tensor_scalar_mul(
            rows[:, 0:512], PS[0:1, 0:512], 1.0 / I),
            reads=[R("PS", 0, 512)], writes=[R("rows", 0, 512)])
        p.add("vector", lambda: nc.vector.tensor_tensor(
            rows[:, 1024:1536], rows[:, 0:512], rows[:, 0:512], ALU.mult),
            reads=[R("rows", 0, 512)], writes=[R("rows", 1024, 1536)])
        p.add("vector", lambda: nc.vector.scalar_tensor_tensor(
            rows[:, 512:1024], PS[0:1, 512:1024], 1.0 / I, rows[:, 1024:1536],
            ALU.mult, ALU.subtract),
            reads=[R("PS", 512, 1024), R("rows", 1024, 1536)],
            writes=[R("rows", 512, 1024)])
        p.add("scalar", lambda: nc.scalar.activation(
            rows[:, 1024:1536], rows[:, 512:1024], AF.Ln, bias=1e-5),
            reads=[R("rows", 512, 1024), R("consts", 0, 1)],
            writes=[R("rows", 1024, 1536)])
        p.add("scalar", lambda: nc.scalar.activation(
            rows[:, 512:1024], rows[:, 1024:1536], AF.Exp, scale=-0.5),
            reads=[R("rows", 1024, 1536), R("consts", 1, 2)],
            writes=[R("rows", 512, 1024)])
        p.add("scalar", lambda: nc.scalar.copy(abbf[:, 0:512], rows[:, 512:1024]),
              reads=[R("rows", 512, 1024)], writes=[R("abbf", 0, 512)])
        p.add("vector", lambda: nc.vector.scalar_tensor_tensor(
            abbf[:, 512:1024], rows[:, 0:512], -1.0, rows[:, 512:1024],
            ALU.mult, ALU.mult),
            reads=[R("rows", 0, 1024)], writes=[R("abbf", 512, 1024)])

        for oc in range(4):
            if oc == 0:
                load_slot(i, ("g", 0))
            lo = next_bank()
            mm_job(wslot(("g", 0), KD, 512, oc), xnf_rhs(), lo, tag=f"g0.{oc}")
            dst, dreg = scr_reg(HE0, oc)
            p.add("scalar",
                  (lambda dst=dst, lo=lo: nc.scalar.activation(
                      dst, PS[:, lo:lo + TT], AF.Silu)),
                  reads=[R("PS", lo, lo + TT)], writes=[dreg])

        p.add("tensor", lambda: nc.tensor.matmul(
            PS[:, 0:512], onesrow[0:1, :], abbf[0:1, 0:512], start=True, stop=True),
            reads=[R("onesrow", 0, 128), R("abbf", 0, 512)],
            writes=[R("PS", 0, 512)], tag="bcA")
        p.add("tensor", lambda: nc.tensor.matmul(
            PS[:, 512:1024], onesrow[0:1, :], abbf[0:1, 512:1024],
            start=True, stop=True),
            reads=[R("onesrow", 0, 128), R("abbf", 512, 1024)],
            writes=[R("PS", 512, 1024)], tag="bcB")
        for oc in range(4):
            if oc == 0:
                load_slot(i, ("u", 0))
            lo = next_bank()
            mm_job(wslot(("u", 0), KD, 512, oc), xnf_rhs(), lo, tag=f"u0.{oc}")
            dst, dreg = scr_reg(SG, oc)
            p.add("scalar",
                  (lambda dst=dst, lo=lo: nc.scalar.copy(dst, PS[:, lo:lo + TT])),
                  reads=[R("PS", lo, lo + TT)], writes=[dreg])
        for oc in range(4):
            if oc == 0:
                load_slot(i, ("g", 1))
            lo = next_bank()
            mm_job(wslot(("g", 1), KD, 512, oc), xnf_rhs(), lo, tag=f"g1.{oc}")
            dst, dreg = scr_reg(HE0 + 2048, oc)
            p.add("scalar",
                  (lambda dst=dst, lo=lo: nc.scalar.activation(
                      dst, PS[:, lo:lo + TT], AF.Silu)),
                  reads=[R("PS", lo, lo + TT)], writes=[dreg])
        for oc in range(4):
            if oc == 0:
                load_slot(i, ("u", 1))
            lo = next_bank()
            mm_job(wslot(("u", 1), KD, 512, oc), xnf_rhs(), lo, tag=f"u1.{oc}")
            dst, dreg = scr_reg(UU, oc)
            p.add("scalar",
                  (lambda dst=dst, lo=lo: nc.scalar.copy(dst, PS[:, lo:lo + TT])),
                  reads=[R("PS", lo, lo + TT)], writes=[dreg])

        for c in range(KI):
            s2 = c % 2
            p.add("vector",
                  (lambda c=c, s2=s2: nc.vector.tensor_tensor(
                      tmp[:, s2 * 512:(s2 + 1) * 512], hbuf[:, c * TT:(c + 1) * TT],
                      PS[:, 0:512], ALU.mult)),
                  reads=[R("hbuf", c * TT, (c + 1) * TT), R("PS", 0, 512)],
                  writes=[R("tmp", s2 * 512, (s2 + 1) * 512)])
            p.add("vector",
                  (lambda c=c, s2=s2: nc.vector.tensor_tensor(
                      hpre[:, c * W + 3: c * W + 3 + TT],
                      tmp[:, s2 * 512:(s2 + 1) * 512], PS[:, 512:1024], ALU.add)),
                  reads=[R("tmp", s2 * 512, (s2 + 1) * 512), R("PS", 512, 1024)],
                  writes=[R("hpre", c * W + 3, c * W + 3 + TT)])

        if i == 0:
            p.add("vector", lambda: nc.vector.tensor_copy(
                lnbbf[:, :], cvec[:, 32:48]),
                reads=[R("cvec_sb", 32, 48)], writes=[R("lnbbf", 0, 16)])
            for j in range(4):
                sidx = SLOT[("fc", j)]
                bb = slot_base(sidx)
                p.add("gpsimd",
                      (lambda sidx=sidx, bb=bb: nc.gpsimd.dma_start(
                          ring[:, bb:bb + SLOTC], wimg_h[sidx])),
                      reads=[R("wimg", sidx * SLOTC, (sidx + 1) * SLOTC)],
                      writes=[R("ring", bb, bb + SLOTC)], dma="q_gp",
                      tag=f"cast{sidx}")
                for jj2 in range(2):
                    oc2 = 2 * j + jj2
                    lo = next_bank()
                    for c in range(KI):
                        la, lreg = slot_ap(("fc", j), c * 256 + jj2 * 128,
                                           c * 256 + (jj2 + 1) * 128)
                        p.add("tensor",
                              (lambda la=la, c=c, lo=lo:
                               nc.tensor.matmul(PS[0:128, lo:lo + 1], la,
                                                lnbbf[:, c:c + 1],
                                                start=(c == 0), stop=(c == KI - 1))),
                              reads=[lreg, R("lnbbf", c, c + 1)],
                              writes=[R("PS", lo, lo + TT)], tag=f"c1.{oc2}")
                    p.add("vector",
                          (lambda oc2=oc2, lo=lo: nc.vector.scalar_tensor_tensor(
                              fbias[:, oc2:oc2 + 1], PS[0:128, lo:lo + 1], 1.0,
                              cvec[:, 48 + oc2:49 + oc2], ALU.mult, ALU.add)),
                          reads=[R("PS", lo, lo + TT),
                                 R("cvec_sb", 48 + oc2, 49 + oc2)],
                          writes=[R("fbias", oc2, oc2 + 1)])
                for c in range(KI):
                    lo2 = c * 256
                    p.add("vector",
                          (lambda j=j, c=c, lo2=lo2: nc.vector.tensor_scalar_mul(
                              slot_ap(("fc", j), lo2, lo2 + 256)[0],
                              slot_ap(("fc", j), lo2, lo2 + 256)[0],
                              cvec[:, 16 + c:17 + c])),
                          reads=[R("ring", bb + lo2, bb + lo2 + 256),
                                 R("cvec_sb", 16 + c, 17 + c)],
                          writes=[R("ring", bb + lo2, bb + lo2 + 256)])
                p.add("sync",
                      (lambda sidx=sidx, bb=bb: nc.sync.dma_start(
                          wsc_h[sidx], ring[:, bb:bb + SLOTC])),
                      reads=[R("ring", bb, bb + SLOTC)],
                      writes=[R("wsc", sidx * SLOTC, (sidx + 1) * SLOTC)],
                      dma="q_sync", tag=f"wb{sidx}")
        hn_rhs = [(hpre[:, c * W + 3: c * W + 3 + TT],
                   R("hpre", c * W + 3, c * W + 3 + TT)) for c in range(KI)]
        for oc in range(8):
            if i != 0 and oc % 2 == 0:
                load_slot(i, ("fc", oc // 2))
            lo = next_bank()
            mm_job(wslot(("fc", oc // 2), KI, 256, oc % 2), hn_rhs, lo,
                   tag=f"fc{oc}")
            p.add("scalar",
                  (lambda oc=oc, lo=lo: nc.scalar.activation(
                      tfc[:, oc * TT:(oc + 1) * TT], PS[:, lo:lo + TT], AF.Silu,
                      bias=fbias[:, oc:oc + 1])),
                  reads=[R("PS", lo, lo + TT), R("fbias", oc, oc + 1)],
                  writes=[R("tfc", oc * TT, (oc + 1) * TT)])

        rot["banks"] = [1024, 1536, 2048, 2560, 512]

        rt3 = rtok[:, :].rearrange("p (s e) -> p s e", e=8)
        l23 = msk[:, 32:64].rearrange("p (s e) -> p s e", e=8)
        mk3 = msk[:, 0:32].rearrange("p (s e) -> p s e", e=8)
        rw3 = rwtok[:, :].rearrange("p (s e) -> p s e", e=8)
        m1b = rsc[:, 0:4].rearrange("p (s o) -> p s o", o=1).broadcast_to([128, 4, 8])
        m2b = rsc[:, 4:8].rearrange("p (s o) -> p s o", o=1).broadcast_to([128, 4, 8])
        p1b = rsc[:, 16:20].rearrange("p (s o) -> p s o", o=1).broadcast_to([128, 4, 8])
        p2b = rsc[:, 20:24].rearrange("p (s o) -> p s o", o=1).broadcast_to([128, 4, 8])
        p.add("vector", lambda: nc.vector.tensor_reduce(
            rsc[:, 0:4], rt3, AX.X, ALU.max),
            reads=[R("rtok", 0, 32)], writes=[R("rsc", 0, 4)])
        p.add("vector", lambda: nc.vector.tensor_tensor(
            mk3, rt3, m1b, ALU.is_equal),
            reads=[R("rtok", 0, 32), R("rsc", 0, 4)], writes=[R("msk", 0, 32)])
        p.add("vector", lambda: nc.vector.scalar_tensor_tensor(
            msk[:, 32:64], msk[:, 0:32], -1e30, rtok[:, :], ALU.mult, ALU.add),
            reads=[R("msk", 0, 32), R("rtok", 0, 32)], writes=[R("msk", 32, 64)])
        p.add("vector", lambda: nc.vector.tensor_reduce(
            rsc[:, 4:8], l23, AX.X, ALU.max),
            reads=[R("msk", 32, 64)], writes=[R("rsc", 4, 8)])
        p.add("vector", lambda: nc.vector.tensor_tensor(
            l23, l23, m2b, ALU.is_equal),
            reads=[R("msk", 32, 64), R("rsc", 4, 8)], writes=[R("msk", 32, 64)])
        p.add("vector", lambda: nc.vector.tensor_sub(
            rsc[:, 8:12], rsc[:, 4:8], rsc[:, 0:4]),
            reads=[R("rsc", 0, 8)], writes=[R("rsc", 8, 12)])
        p.add("scalar", lambda: nc.scalar.activation(
            rsc[:, 12:16], rsc[:, 8:12], AF.Exp),
            reads=[R("rsc", 8, 12)], writes=[R("rsc", 12, 16)])
        if i + 1 < n_tiles:
            ln1_stats(i + 1)
            ln1_apply_a(i + 1)
        p.add("vector", lambda: nc.vector.tensor_scalar_add(
            rsc[:, 16:20], rsc[:, 12:16], 1.0),
            reads=[R("rsc", 12, 16)], writes=[R("rsc", 16, 20)])
        p.add("vector", lambda: nc.vector.reciprocal(
            rsc[:, 16:20], rsc[:, 16:20]),
            reads=[R("rsc", 16, 20)], writes=[R("rsc", 16, 20)])
        p.add("vector", lambda: nc.vector.tensor_tensor(
            rsc[:, 20:24], rsc[:, 12:16], rsc[:, 16:20], ALU.mult),
            reads=[R("rsc", 12, 20)], writes=[R("rsc", 20, 24)])
        p.add("vector", lambda: nc.vector.tensor_tensor(
            rw3, mk3, p1b, ALU.mult),
            reads=[R("msk", 0, 32), R("rsc", 16, 20)], writes=[R("rwtok", 0, 32)])
        p.add("vector", lambda: nc.vector.tensor_tensor(
            mk3, l23, p2b, ALU.mult),
            reads=[R("msk", 32, 64), R("rsc", 20, 24)], writes=[R("msk", 0, 32)])
        p.add("vector", lambda: nc.vector.tensor_add(
            rwtok[:, :], rwtok[:, :], msk[:, 0:32]),
            reads=[R("rwtok", 0, 32), R("msk", 0, 32)], writes=[R("rwtok", 0, 32)])

        tfc_rhs = [(tfc[:, c * TT:(c + 1) * TT], R("tfc", c * TT, (c + 1) * TT))
                   for c in range(KD)]
        for oc in range(KI):
            if oc % 4 == 0:
                load_slot(i, ("cp", oc // 4))
            lo = next_bank()
            mm_job(wslot(("cp", oc // 4), KD, 512, oc % 4), tfc_rhs, lo,
                   tag=f"cp{oc}")
            p.add("vector",
                  (lambda oc=oc, lo=lo: nc.vector.scalar_tensor_tensor(
                      gate[:, oc * TT:(oc + 1) * TT], PS[:, lo:lo + TT],
                      cvec[:, 56 + oc:57 + oc], hbuf[:, oc * TT:(oc + 1) * TT],
                      ALU.add, ALU.add)),
                  reads=[R("PS", lo, lo + TT), R("cvec_sb", 56 + oc, 57 + oc),
                         R("hbuf", oc * TT, (oc + 1) * TT)],
                  writes=[R("gate", oc * TT, (oc + 1) * TT)])

        lo_t2 = next_bank()
        for s in range(4):
            p.add("tensor",
                  (lambda s=s, lo=lo_t2: nc.tensor.transpose(
                      PS[0:8, lo + s * 128:lo + (s + 1) * 128],
                      rwtok[:, s * 8:(s + 1) * 8], iden32[:, :])),
                  reads=[R("rwtok", s * 8, (s + 1) * 8), R("iden32", 0, 128)],
                  writes=[R("PS", lo_t2 + s * 128, lo_t2 + (s + 1) * 128)])
        p.add("scalar",
              (lambda lo=lo_t2: nc.scalar.copy(rwf[:, :], PS[0:8, lo:lo + TT])),
              reads=[R("PS", lo_t2, lo_t2 + TT)], writes=[R("rwf", 0, TT)])

        def emit_rw_tt(ex, ubase):
            heb = HE0 + (ex % 2) * 2048
            p.add("tensor",
                  (lambda ex=ex: nc.tensor.matmul(
                      PS[:, 0:512], eh[:, ex * 128:(ex + 1) * 128], rwf[0:8, :],
                      start=True, stop=True)),
                  reads=[R("eh", ex * 128, (ex + 1) * 128), R("rwf", 0, TT)],
                  writes=[R("PS", 0, 512)], tag=f"rw{ex}")
            for oc in range(4):
                ha, hreg = scr_reg(heb, oc)
                p.add("vector",
                      (lambda ha=ha, hreg=hreg: nc.vector.tensor_tensor(
                          ha, ha, PS[:, 0:512], ALU.mult)),
                      reads=[hreg, R("PS", 0, 512)], writes=[hreg])
            for oc in range(4):
                ha, hreg = scr_reg(heb, oc)
                ua, ureg = scr_reg(ubase, oc)
                p.add("vector",
                      (lambda ha=ha, ua=ua, hreg=hreg, ureg=ureg:
                       nc.vector.tensor_tensor(ha, ha, ua, ALU.mult)),
                      reads=[hreg, ureg], writes=[hreg])

        def emit_wd(ex):
            heb = HE0 + (ex % 2) * 2048
            he_rhs = [scr_reg(heb, c) for c in range(4)]
            for oc in range(8):
                if oc == 0:
                    load_slot(i, ("d", ex))
                lo = next_bank()
                mm_job(wslot(("d", ex), 4, 1024, oc), he_rhs, lo, tag=f"d{ex}.{oc}")
                p.add("vector",
                      (lambda oc=oc, lo=lo: nc.vector.tensor_tensor(
                          yacc[:, oc * TT:(oc + 1) * TT],
                          yacc[:, oc * TT:(oc + 1) * TT],
                          PS[:, lo:lo + TT], ALU.add)),
                      reads=[R("yacc", oc * TT, (oc + 1) * TT), R("PS", lo, lo + TT)],
                      writes=[R("yacc", oc * TT, (oc + 1) * TT)])

        emit_rw_tt(0, SG)

        h2_rhs = [(gate[:, c * TT:(c + 1) * TT], R("gate", c * TT, (c + 1) * TT))
                  for c in range(KI)]
        for oc in range(8):
            if oc % 2 == 0:
                load_slot(i, ("ow", oc // 2))
            lo = next_bank()
            mm_job(wslot(("ow", oc // 2), KI, 256, oc % 2), h2_rhs, lo,
                   tag=f"ow{oc}")
            p.add("scalar",
                  (lambda oc=oc, lo=lo: nc.scalar.copy(
                      yacc[:, oc * TT:(oc + 1) * TT], PS[:, lo:lo + TT])),
                  reads=[R("PS", lo, lo + TT)],
                  writes=[R("yacc", oc * TT, (oc + 1) * TT)])

        emit_rw_tt(1, UU)
        for ex in range(2, NE):
            heb = HE0 + (ex % 2) * 2048
            for oc in range(4):
                if oc == 0:
                    load_slot(i, ("g", ex))
                lo = next_bank()
                mm_job(wslot(("g", ex), KD, 512, oc), xnf_rhs(), lo,
                       tag=f"g{ex}.{oc}")
                dst, dreg = scr_reg(SG, oc)
                p.add("scalar",
                      (lambda dst=dst, lo=lo: nc.scalar.activation(
                          dst, PS[:, lo:lo + TT], AF.Silu)),
                      reads=[R("PS", lo, lo + TT)], writes=[dreg])
            for oc in range(4):
                if oc == 0:
                    load_slot(i, ("u", ex))
                lo = next_bank()
                mm_job(wslot(("u", ex), KD, 512, oc), xnf_rhs(), lo,
                       tag=f"u{ex}.{oc}")
                dst, dreg = scr_reg(UU, oc)
                p.add("scalar",
                      (lambda dst=dst, lo=lo: nc.scalar.copy(dst, PS[:, lo:lo + TT])),
                      reads=[R("PS", lo, lo + TT)], writes=[dreg])
            emit_wd(ex - 2)
            p.add("tensor",
                  (lambda ex=ex: nc.tensor.matmul(
                      PS[:, 0:512], eh[:, ex * 128:(ex + 1) * 128], rwf[0:8, :],
                      start=True, stop=True)),
                  reads=[R("eh", ex * 128, (ex + 1) * 128), R("rwf", 0, TT)],
                  writes=[R("PS", 0, 512)], tag=f"rw{ex}")
            for oc in range(4):
                sa, sreg = scr_reg(SG, oc)
                ha, hreg = scr_reg(heb, oc)
                p.add("vector",
                      (lambda sa=sa, ha=ha, sreg=sreg, hreg=hreg:
                       nc.vector.tensor_tensor(ha, sa, PS[:, 0:512], ALU.mult)),
                      reads=[sreg, R("PS", 0, 512)], writes=[hreg])
            for oc in range(4):
                ha, hreg = scr_reg(heb, oc)
                ua, ureg = scr_reg(UU, oc)
                p.add("vector",
                      (lambda ha=ha, ua=ua, hreg=hreg, ureg=ureg:
                       nc.vector.tensor_tensor(ha, ha, ua, ALU.mult)),
                      reads=[hreg, ureg], writes=[hreg])
        emit_wd(NE - 2)
        emit_wd(NE - 1)

        for oc in range(8):
            if oc % 4 == 0:
                load_slot(i, ("sg", oc // 4))
            lo = next_bank()
            mm_job(wslot(("sg", oc // 4), KD, 512, oc % 4), xnf_rhs(), lo,
                   tag=f"sg{oc}")
            p.add("scalar",
                  (lambda oc=oc, lo=lo: nc.scalar.activation(
                      scr[:, oc * TT:(oc + 1) * TT], PS[:, lo:lo + TT], AF.Silu)),
                  reads=[R("PS", lo, lo + TT)],
                  writes=[R("scr", oc * TT, (oc + 1) * TT)])
        for oc in range(8):
            if oc % 4 == 0:
                load_slot(i, ("su", oc // 4))
            lo = next_bank()
            mm_job(wslot(("su", oc // 4), KD, 512, oc % 4), xnf_rhs(), lo,
                   tag=f"su{oc}")
            p.add("vector",
                  (lambda oc=oc, lo=lo: nc.vector.tensor_tensor(
                      scr[:, 4096 + oc * TT:4096 + (oc + 1) * TT],
                      scr[:, oc * TT:(oc + 1) * TT], PS[:, lo:lo + TT], ALU.mult)),
                  reads=[R("scr", oc * TT, (oc + 1) * TT), R("PS", lo, lo + TT)],
                  writes=[R("scr", 4096 + oc * TT, 4096 + (oc + 1) * TT)])

        prod_rhs = [(scr[:, 4096 + c * TT:4096 + (c + 1) * TT],
                     R("scr", 4096 + c * TT, 4096 + (c + 1) * TT))
                    for c in range(KD)]
        for oc in range(8):
            if oc % 4 == 0:
                load_slot(i, ("sd", oc // 4))
            if i + 1 < n_tiles and oc < 4:
                ln1_apply(i + 1, [oc])
            lo = next_bank()
            mm_job(wslot(("sd", oc // 4), KD, 512, oc % 4), prod_rhs, lo,
                   tag=f"sd{oc}")
            p.add("vector",
                  (lambda oc=oc, lo=lo: nc.vector.tensor_tensor(
                      yacc[:, oc * TT:(oc + 1) * TT],
                      yacc[:, oc * TT:(oc + 1) * TT],
                      PS[:, lo:lo + TT], ALU.add)),
                  reads=[R("yacc", oc * TT, (oc + 1) * TT), R("PS", lo, lo + TT)],
                  writes=[R("yacc", oc * TT, (oc + 1) * TT)])
            p.add("scalar",
                  (lambda oc=oc: nc.scalar.copy(
                      scr[:, oc * TT:(oc + 1) * TT], yacc[:, oc * TT:(oc + 1) * TT])),
                  reads=[R("yacc", oc * TT, (oc + 1) * TT)],
                  writes=[R("scr", oc * TT, (oc + 1) * TT)])

        def make_yfin(i=i):
            par = i % 2
            for s in range(4):
                half = s % 2
                for c in range(8):
                    p.add("tensor",
                          (lambda s=s, c=c, half=half: nc.tensor.transpose(
                              PT[:, half * 1024 + c * 128: half * 1024 + (c + 1) * 128],
                              scr[:, c * TT + s * 128: c * TT + (s + 1) * 128],
                              iden[:, :])),
                          reads=[R("scr", c * TT + s * 128, c * TT + (s + 1) * 128),
                                 R("iden", 0, 128)],
                          writes=[R("PT", half * 1024 + c * 128,
                                    half * 1024 + (c + 1) * 128)])
                p.add("vector",
                      (lambda i=i, s=s, half=half: nc.vector.tensor_tensor(
                          xsub(i, s), xsub(i, s),
                          PT[:, half * 1024:(half + 1) * 1024], ALU.add)),
                      reads=[xreg(i, s), R("PT", half * 1024, (half + 1) * 1024)],
                      writes=[xreg(i, s)])
            dst = y_h[i * TT:(i + 1) * TT, :].rearrange("(s p) d -> p s d", p=128)
            ysrc = xtok[:, par * 4096:(par + 1) * 4096].rearrange(
                "p (s d) -> p s d", d=D)
            p.add("gpsimd",
                  (lambda dst=dst, ysrc=ysrc: nc.gpsimd.dma_start(dst, ysrc)),
                  reads=[R("xtok", par * 4096, (par + 1) * 4096)],
                  writes=[R("y_h", i * TT * D, (i + 1) * TT * D)],
                  dma="q_gp", tag=f"y{i}")
        pend_yfin[0] = make_yfin

    if pend_yfin[0] is not None:
        pend_yfin[0]()
        pend_yfin[0] = None
    counts = dict(p.dma_counts)
    p.add("gpsimd", lambda: nc.gpsimd.nop(), reads=[R("y_h", 0, T * D)],
          tag="final")
    p.finalize()

    per_engine = {en: [op for op in p.ops if op["eng"] == en] for en in ENGINES}

    with ctx:
        with nc.Block() as block:
            def emit_stream(en):
                eng_obj = getattr(nc, en)
                for op in per_engine[en]:
                    for sem_name, val in op["waits"]:
                        eng_obj.wait_ge(sems[sem_name], val)
                    inst = op["emit"]()
                    if op["dma"] is not None:
                        inst.then_inc(sems[op["dma"]], 16)
                    elif op["inc"]:
                        inst.then_inc(sems["e_" + en], 1)

            @block.tensor
            def _(t):
                emit_stream("tensor")

            @block.vector
            def _(v):
                emit_stream("vector")

            @block.scalar
            def _(s):
                emit_stream("scalar")

            @block.sync
            def _(sy):
                emit_stream("sync")
                for ln in range(NLANE):
                    sy.wait_ge(sems[f"q_sync{ln}"], counts.get(f"q_sync{ln}", 0))

            @block.gpsimd
            def _(g):
                emit_stream("gpsimd")
                for ln in range(NLANE):
                    g.wait_ge(sems[f"q_gp{ln}"], counts.get(f"q_gp{ln}", 0))
    return nc



def _img(Wm, kchunks, sl=None):
    d_in, d_out = Wm.shape
    assert d_in == kchunks * 128
    V = Wm.reshape(kchunks, 128, d_out)
    if sl is not None:
        V = V[:, :, sl]
    return np.ascontiguousarray(V.transpose(1, 0, 2).reshape(128, -1))


def _prep(weights):
    (ipw, convw, convb, lng, lnb, fcw, fcb, cpw, cpb, oww,
     wr, wg, wu, wd, sg, su, sd, eb) = weights
    imgs = {}
    ipT = np.ascontiguousarray(ipw.T)
    for j in range(8):
        imgs[("ip", j)] = _img(ipT, 8, slice(j * 512, (j + 1) * 512))
    for j in range(4):
        img = np.empty((128, 4096), np.float32)
        for cc in range(2):
            for k in range(4):
                for og in range(2):
                    for oc2 in range(2):
                        col = cc * 2048 + k * 512 + og * 256 + oc2 * 128
                        blk = convw[(2 * j + og) * 256 + oc2 * 128:
                                    (2 * j + og) * 256 + (oc2 + 1) * 128,
                                    cc * 128:(cc + 1) * 128, k]
                        img[:, col:col + 128] = blk.T
        imgs[("cv", j)] = img
    fcT = np.ascontiguousarray(fcw.T)
    for j in range(4):
        imgs[("fc", j)] = _img(fcT, 16, slice(j * 256, (j + 1) * 256))
    cpT = np.ascontiguousarray(cpw.T)
    for j in range(4):
        imgs[("cp", j)] = _img(cpT, 8, slice(j * 512, (j + 1) * 512))
    owT = np.ascontiguousarray(oww.T)
    for j in range(4):
        imgs[("ow", j)] = _img(owT, 16, slice(j * 256, (j + 1) * 256))
    for ex in range(8):
        imgs[("g", ex)] = _img(wg[ex], 8)
        imgs[("u", ex)] = _img(wu[ex], 8)
        imgs[("d", ex)] = _img(wd[ex], 4)
    for j in range(2):
        imgs[("sg", j)] = _img(sg, 8, slice(j * 512, (j + 1) * 512))
        imgs[("su", j)] = _img(su, 8, slice(j * 512, (j + 1) * 512))
        imgs[("sd", j)] = _img(sd, 8, slice(j * 512, (j + 1) * 512))
    wimg = np.stack([imgs[k] for k in SLOT_ORDER], axis=0)
    assert wimg.shape == (NSLOT, 128, SLOTC), wimg.shape

    cvecs = np.zeros((128, 80), np.float32)
    cvecs[:, 0:16] = convb.reshape(16, 128).T
    cvecs[:, 16:32] = lng.reshape(16, 128).T
    cvecs[:, 32:48] = lnb.reshape(16, 128).T
    cvecs[:, 48:56] = fcb.reshape(8, 128).T
    cvecs[:, 56:72] = cpb.reshape(16, 128).T
    cvecs[0:8, 72] = eb
    wrimg = np.zeros((128, 64), np.float32)
    for c in range(8):
        wrimg[:, c * 8:(c + 1) * 8] = wr[c * 128:(c + 1) * 128, :]
    return wimg, cvecs, wrimg


def kernel(L0_attn_hidden_states, L0_attn_in_proj_w, L0_attn_conv_w,
           L0_attn_conv_b, L0_attn_ln_g, L0_attn_ln_b, L0_attn_fc_w,
           L0_attn_fc_b, L0_attn_cproj_w, L0_attn_cproj_b, L0_attn_out_w,
           L0_ffn_Wr, L0_ffn_Wg, L0_ffn_Wu, L0_ffn_Wd, L0_ffn_Sg, L0_ffn_Su,
           L0_ffn_Sd, L0_ffn_expert_bias):
    x = np.ascontiguousarray(np.asarray(L0_attn_hidden_states, dtype=np.float32))
    B, T, _ = x.shape
    weights = [np.ascontiguousarray(np.asarray(w_, dtype=np.float32)) for w_ in (
        L0_attn_in_proj_w, L0_attn_conv_w, L0_attn_conv_b, L0_attn_ln_g,
        L0_attn_ln_b, L0_attn_fc_w, L0_attn_fc_b, L0_attn_cproj_w,
        L0_attn_cproj_b, L0_attn_out_w, L0_ffn_Wr, L0_ffn_Wg, L0_ffn_Wu,
        L0_ffn_Wd, L0_ffn_Sg, L0_ffn_Su, L0_ffn_Sd, L0_ffn_expert_bias)]
    wimg, cvecs, wrimg = _prep(weights)
    nc = build_nc(T // TT)
    in_maps = [{"x": x[c], "wimg": wimg, "cvec": cvecs, "wrimg": wrimg}
               for c in range(B)]
    res = run_bass_kernel_spmd(nc, in_maps, core_ids=list(range(B)))
    return np.stack([r["y"] for r in res.results], axis=0)
